# version 01
# baseline (speedup 1.0000x reference)
import numpy as np
from contextlib import ExitStack
import concourse.bass as bass
import concourse.mybir as mybir
from concourse.bass_utils import run_bass_kernel_spmd

F32 = mybir.dt.float32
F32R = mybir.dt.float32r
BF16 = mybir.dt.bfloat16
I32 = mybir.dt.int32
AF = mybir.ActivationFunctionType
ALU = mybir.AluOpType

T = 4096
DM = 1024
NCORES = 8
TT = 512
NTT = T // TT

ENGINES = ("tensor", "vector", "scalar", "gpsimd", "sync")


class Sem:
    __slots__ = ("h", "n", "name")

    def __init__(self, h, name):
        self.h = h
        self.n = 0
        self.name = name

    def bump(self, k):
        self.n += k
        return self.n


class Prog:
    def __init__(self, nc, debug=False):
        self.nc = nc
        self.debug = debug
        self.stack = ExitStack()
        self.sstack = None
        self.stage_name = ""
        self.q = {e: [] for e in ENGINES}
        self.pending = {e: [] for e in ENGINES}
        self._sems = {}
        self._sb = {}
        self._dram = {}
        self.ps = self.stack.enter_context(nc.psum_tensor("psum_pool", [128, 4096], F32))
        self.ones = self.stack.enter_context(nc.sbuf_tensor("c_ones", [128, 128], F32R, side="right"))
        self.onesf = self.stack.enter_context(nc.sbuf_tensor("c_onesf", [128, 128], F32, side="right"))
        self.zrow = self.stack.enter_context(
            nc.sbuf_tensor("c_zrow", [2, 4096], F32, side="right"))
        self._cc = {}
        self._dma_sems = set()
        ic = self.sem("init")
        self.rec("vector", lambda e: e.memset(self.onesf[:], 1.0))
        self.rec("vector", lambda e: e.memset(self.zrow[:], 0.0))
        self.rec("vector", lambda e: e.tensor_copy(self.ones[:], self.onesf[:]), inc=ic)
        for e in ENGINES:
            self.pending[e].append((ic, 1))

    def zero2(self, ap, inc=None):
        n = ap.shape[1]
        return self.rec("vector",
                        lambda e, o=ap, z=self.zrow[0:2, 0:n]: e.tensor_copy(o, z), inc=inc)

    def constcol(self, val):
        key = float(np.float32(val))
        if key not in self._cc:
            t = self.stack.enter_context(
                self.nc.sbuf_tensor(f"cc{len(self._cc)}", [128, 1], F32, side="right"))
            s = self.sem("cc")
            v = self.rec("vector", lambda e, tt=t, vv=key: e.memset(tt[:], vv), inc=s)
            for e in ENGINES:
                self.pending[e].append((s, v))
            self._cc[key] = t
        return self._cc[key][:, 0:1]

    def sem(self, name):
        if name not in self._sems:
            h = self.stack.enter_context(self.nc.semaphore(name))
            self._sems[name] = Sem(h, name)
        return self._sems[name]

    def sb(self, name, shape, dtype):
        if name not in self._sb:
            self._sb[name] = self.stack.enter_context(
                self.nc.sbuf_tensor(name, shape, dtype, side="right"))
        return self._sb[name]

    def scratch(self, name, shape, dtype):
        assert self.sstack is not None, "scratch outside stage"
        return self.sstack.enter_context(
            self.nc.sbuf_tensor(f"{self.stage_name}_{name}", shape, dtype))

    def dram(self, name, shape, dtype, kind="Internal"):
        if name not in self._dram:
            k = kind
            if kind == "Internal" and self.debug:
                k = "ExternalOutput"
            self._dram[name] = self.nc.dram_tensor(name, list(shape), dtype, kind=k).ap()
        return self._dram[name]

    def bank(self, i, n=512):
        return self.ps[:, i * 512:i * 512 + n]

    def rec(self, eng, fn, waits=(), inc=None, inc_n=1, dma=False):
        pw = self.pending[eng]
        if pw:
            self.pending[eng] = []
        tgt = inc.bump(inc_n) if inc is not None else None
        ih = inc.h if inc is not None else None
        wl = [(s.h, v) for (s, v) in list(pw) + list(waits) if s is not None]
        auto_drain = (not dma) and eng in ("vector", "gpsimd")
        needs_drain = (inc is not None and not dma
                       and eng in ("vector", "scalar", "gpsimd"))

        def run(e, fn=fn, wl=wl, ih=ih, inc_n=inc_n, nd=needs_drain, ad=auto_drain):
            for h, v in wl:
                e.wait_ge(h, v)
            ins = fn(e)
            if ad:
                e.drain()
            elif ih is not None and nd:
                e.drain()
            if ih is not None:
                if nd or ad:
                    e.nop().then_inc(ih, inc_n)
                else:
                    ins.then_inc(ih, inc_n)
        self.q[eng].append(run)
        return tgt

    def dmasem(self, name):
        s = self.sem("dma_" + name)
        self._dma_sems.add("dma_" + name)
        return s

    def load(self, queue, dst, src, waits=(), sem="g"):
        s = self.dmasem(sem)
        v = self.rec(queue, lambda e, d=dst, so=src: e.dma_start(out=d, in_=so),
                     waits=waits, inc=s, inc_n=16, dma=True)
        return (s, v)

    def store(self, queue, dst, src, waits=(), accum=False, sem="gs"):
        s = self.dmasem(sem)
        if accum:
            fn = lambda e, d=dst, so=src: e.dma_start(out=d, in_=so, accum_op=ALU.add)
        else:
            fn = lambda e, d=dst, so=src: e.dma_start(out=d, in_=so)
        v = self.rec(queue, fn, waits=waits, inc=s, inc_n=16, dma=True)
        return (s, v)

    def mm(self, out, lhsT, rhs, start, stop, waits=(), inc=None, tile_position=None):
        return self.rec(
            "tensor",
            lambda e, o=out, l=lhsT, r=rhs, st=start, sp=stop, tp=tile_position:
                e.matmul(o, l, r, start=st, stop=sp, tile_position=tp),
            waits=waits, inc=inc)

    def trans(self, out, in_, ident, waits=(), inc=None):
        return self.rec(
            "tensor",
            lambda e, o=out, i=in_, idn=ident: e.transpose(o, i, idn),
            waits=waits, inc=inc)

    def act(self, out, in_, func, bias=0.0, scale=1.0, waits=(), inc=None, eng="scalar"):
        if isinstance(bias, (int, float)) and float(bias) != 0.0:
            bias = self.constcol(bias)
        if isinstance(scale, (int, float)) and float(scale) != 1.0:
            scale = self.constcol(scale)
        return self.rec(
            eng,
            lambda e, o=out, i=in_, f=func, b=bias, s=scale: e.activation(o, i, f, bias=b, scale=s),
            waits=waits, inc=inc)

    def vop(self, fn, waits=(), inc=None, eng="vector"):
        return self.rec(eng, fn, waits=waits, inc=inc)

    def barrier(self):
        bar = self.sem("bar")
        dma_waits = [(self._sems[n], self._sems[n].n) for n in sorted(self._dma_sems)]
        for e in ("vector", "scalar", "gpsimd"):
            self.rec(e, lambda eng: eng.drain())
        self.rec("sync", lambda e: e.nop(), waits=dma_waits, inc=bar)
        self.rec("scalar", lambda e: e.nop(), inc=bar)
        self.rec("gpsimd", lambda e: e.nop(), inc=bar)
        self.rec("vector", lambda e: e.nop(), inc=bar)
        tgt = bar.n
        for e in ENGINES:
            self.pending[e].append((bar, tgt))

    def fence(self, engines=("vector", "scalar")):
        s = self.sem("fence")
        for e in engines:
            if e in ("vector", "scalar", "gpsimd"):
                self.rec(e, lambda eng: eng.drain())
            self.rec(e, lambda eng: eng.nop(), inc=s)
        tgt = s.n
        for e in ENGINES:
            self.pending[e].append((s, tgt))

    def begin_stage(self, name):
        if self.sstack is not None:
            self.end_stage()
        self.barrier()
        self.sstack = ExitStack()
        self.stage_name = name

    def end_stage(self):
        if self.sstack is not None:
            self.sstack.close()
            self.sstack = None

    def finish(self):
        self.end_stage()
        self.barrier()
        fin = self.sem("fin")
        self.rec("sync", lambda e: e.nop(), inc=fin)

        def clear_all(e):
            for s in self._sems.values():
                e.sem_clear(s.h)
            return e.nop()
        self.rec("gpsimd", clear_all, waits=[(fin, 1)])
        nc = self.nc
        with nc.Block() as block:
            @block.tensor
            def _(eng):
                for f in self.q["tensor"]:
                    f(eng)

            @block.vector
            def _(eng):
                for f in self.q["vector"]:
                    f(eng)

            @block.scalar
            def _(eng):
                for f in self.q["scalar"]:
                    f(eng)

            @block.gpsimd
            def _(eng):
                for f in self.q["gpsimd"]:
                    f(eng)

            @block.sync
            def _(eng):
                for f in self.q["sync"]:
                    f(eng)



def load_bias_cols(P, bias_dram, nchunks, name="bias"):
    row = P.scratch(name + "_row", [2, nchunks * 128], F32R)
    cols = P.scratch(name + "_cols", [128, nchunks], F32)
    zs = P.sem("ev_util")
    zv = P.rec("vector", lambda e, o=row[0:2, :], z=P.zrow[0:2, 0:nchunks * 128]:
               e.tensor_copy(o, z), inc=zs)
    if len(bias_dram.shape) == 1:
        bsrc = bias_dram.rearrange("(a o) -> a o", a=1)
    else:
        bsrc = bias_dram
    lv = P.load("gpsimd", row[0:1, :], bsrc, waits=[(zs, zv)], sem="misc")
    mms = P.sem("mm_util")
    evs = P.sem("ev_util")
    for c in range(nchunks):
        off = 7 * 512 + 384 + 2 * (c % 2)
        b = P.ps[:, off:off + 2]
        wv = [lv] if c == 0 else []
        if c >= 1:
            wv.append((evs, evs.n))
        mv = P.mm(b, row[0:2, c * 128:(c + 1) * 128], P.ones[0:2, 0:2],
                  start=True, stop=True, waits=wv, inc=mms)
        P.vop(lambda e, o=cols[:, c:c + 1], i=b[:, 0:1]: e.tensor_copy(o, i),
              waits=[(mms, mv)], inc=evs)
    P.fence(("vector",))
    return cols, (evs, evs.n)


def make_identity(P):
    ident = P.scratch("ident", [128, 128], F32)
    dr = P.sem("gp_done")
    P.vop(lambda e: e.iota(ident[:], [[-1, 128]], base=0, channel_multiplier=1,
                           allow_small_or_imprecise_dtypes=True), eng="gpsimd")
    P.rec("gpsimd", lambda e: e.drain())
    P.rec("gpsimd", lambda e: e.nop(), inc=dr)
    P.pending["vector"].append((dr, dr.n))
    iv = P.vop(lambda e: e.tensor_single_scalar(ident[:], ident[:], 0.0, ALU.is_equal),
               inc=P.sem("ev_util"))
    P.pending["tensor"].append((P.sem("ev_util"), iv))
    return ident



def emit_layernorm(P, name, x_dram, out_dram, gamma_dram=None, beta_dram=None,
                   eps=1e-5, D=DM, dbg=None):
    nd = D // 128
    P.begin_stage(name)
    gcols = bcols = None
    if gamma_dram is not None:
        gcols, _ = load_bias_cols(P, gamma_dram, nd, "g")
        bcols, _ = load_bias_cols(P, beta_dram, nd, "b")
    xblk = P.scratch("xblk", [128, 2 * nd * TT], F32R)
    x2 = P.scratch("x2", [128, nd * TT], F32R)
    rows = P.scratch("rows", [1, 3 * TT], F32)
    rowr = P.scratch("rowr", [2, 2 * TT], F32R)
    stg = P.scratch("stg", [128, 2 * nd * TT], F32R)
    mms = P.sem("mm_s")
    evs = P.sem("ev_s")
    sqs = P.sem("sq_s")
    P.zero2(rowr[0:2, :])
    last_apply = 0
    last_bm = None
    last_store = {}
    for t in range(NTT):
        buf = (t % 2) * nd * TT
        ldv = []
        for dc in range(nd):
            qn = ("sync", "scalar")[dc % 2]
            wv = [(evs, last_apply)] if t >= 2 else []
            ldv.append(P.load(qn, xblk[:, buf + dc * TT: buf + (dc + 1) * TT],
                              x_dram[dc * 128:(dc + 1) * 128, t * TT:(t + 1) * TT],
                              waits=wv, sem=f"x{(t % 2) * 8 + dc}"))
        sqv = []
        for dc in range(nd):
            sqv.append(P.act(x2[:, dc * TT:(dc + 1) * TT],
                             xblk[:, buf + dc * TT: buf + (dc + 1) * TT],
                             AF.Square, waits=[ldv[dc], (evs, last_apply)], inc=sqs))
        for dc in range(nd):
            P.mm(P.bank(0)[0:1, :], P.ones[:, 0:1], xblk[:, buf + dc * TT: buf + (dc + 1) * TT],
                 start=(dc == 0), stop=(dc == nd - 1),
                 waits=[ldv[dc], (evs, evs.n)] if dc == 0 else [ldv[dc]])
        mv = None
        for dc in range(nd):
            mv = P.mm(P.bank(1)[0:1, :], P.ones[:, 0:1], x2[:, dc * TT:(dc + 1) * TT],
                      start=(dc == 0), stop=(dc == nd - 1),
                      waits=[(sqs, sqv[dc])], inc=mms if dc == nd - 1 else None)
        r_mean = rows[0:1, 0:TT]
        r_msq = rows[0:1, TT:2 * TT]
        r_var = rows[0:1, 2 * TT:3 * TT]
        rr_mean = rowr[0:1, 0:TT]
        rr_rstd = rowr[0:1, TT:2 * TT]
        P.vop(lambda e, o=r_mean, i=P.bank(0)[0:1, :]: e.tensor_scalar_mul(o, i, 1.0 / D),
              waits=[(mms, mv)])
        P.vop(lambda e, o=r_msq, i=P.bank(1)[0:1, :]: e.tensor_scalar_mul(o, i, 1.0 / D))
        P.vop(lambda e, o=r_var, a=r_mean, b=r_mean: e.tensor_tensor(o, a, b, ALU.mult))
        P.vop(lambda e, o=r_var, a=r_msq, b=r_var: e.tensor_tensor(o, a, b, ALU.subtract))
        P.vop(lambda e, o=r_var, i=r_var: e.tensor_scalar_add(o, i, eps))
        rcv = P.vop(lambda e, o=r_var, i=r_var: e.reciprocal(o, i), inc=evs)
        wv_bm = [] if last_bm is None else [(mms, last_bm)]
        vv = P.vop(lambda e, o=rr_mean, i=r_mean: e.tensor_copy(o, i),
                   waits=wv_bm, inc=evs)
        sv = P.act(rr_rstd, r_var, AF.Sqrt, waits=[(evs, rcv)] + wv_bm, inc=sqs)
        P.mm(P.bank(2), P.ones[0:2, :], rowr[0:2, 0:TT], start=True, stop=True,
             waits=[(evs, vv), (sqs, sv)])
        bm2 = P.mm(P.bank(3), P.ones[0:2, :], rowr[0:2, TT:2 * TT], start=True, stop=True, inc=mms)
        last_bm = bm2
        if dbg is not None:
            dbr = P.scratch(f"dbr{t}", [1, 4 * TT], F32)
            P.vop(lambda e, o=dbr[0:1, 0:TT], i=P.bank(0)[0:1, :]: e.tensor_copy(o, i),
                  waits=[(mms, bm2)])
            P.vop(lambda e, o=dbr[0:1, TT:2 * TT], i=P.bank(2)[0:1, :]: e.tensor_copy(o, i))
            P.vop(lambda e, o=dbr[0:1, 2 * TT:3 * TT], i=P.bank(2)[64:65, :]: e.tensor_copy(o, i))
            dvv = P.vop(lambda e, o=dbr[0:1, 3 * TT:4 * TT], i=rowr[0:1, 0:TT].bitcast(F32):
                        e.tensor_copy(o, i), inc=evs)
            for j in range(4):
                sv1 = P.store("sync", dbg[j:j + 1, t * TT:(t + 1) * TT],
                              dbr[0:1, j * TT:(j + 1) * TT], waits=[(evs, dvv)], sem="dbgs")
            P.pending["vector"].append(sv1)
            P.pending["tensor"].append(sv1)
        for dc in range(nd):
            xa = xblk[:, buf + dc * TT: buf + (dc + 1) * TT].bitcast(F32)
            sa = stg[:, buf + dc * TT: buf + (dc + 1) * TT]
            wv = [(mms, bm2)]
            if dc in last_store:
                wv.append(last_store[dc])
            P.vop(lambda e, o=sa, a=xa, b=P.bank(2): e.tensor_tensor(o, a, b, ALU.subtract),
                  waits=wv)
            if gcols is None:
                la = P.vop(lambda e, o=sa, a=sa.bitcast(F32), b=P.bank(3):
                           e.tensor_tensor(o, a, b, ALU.mult), inc=evs)
            else:
                P.vop(lambda e, o=sa, a=sa.bitcast(F32), b=P.bank(3):
                      e.tensor_tensor(o, a, b, ALU.mult))
                la = P.vop(lambda e, o=sa, i=sa.bitcast(F32), g=gcols[:, dc:dc + 1],
                           b=bcols[:, dc:dc + 1]:
                           e.tensor_scalar(o, i, g, b, ALU.mult, ALU.add), inc=evs)
            last_store[dc] = P.store("sync",
                                     out_dram[dc * 128:(dc + 1) * 128, t * TT:(t + 1) * TT],
                                     sa, waits=[(evs, la)], sem=f"so{(t % 2) * 8 + dc}")
        last_apply = evs.n
    P.end_stage()



def emit_proj(P, name, w_dram, x_dram, out_dram, D, O, epi=None, Tn=T,
              out_row0=0, accum_out=False):
    if epi is None:
        epi = {}
    PD = 128 if D % 128 == 0 else D
    assert D % PD == 0 and (PD == 128 or PD in (64,))
    nd = D // PD
    nog = O // TT
    ntt = Tn // TT
    P.begin_stage(name)
    bias_cols = None
    if epi.get("bias") is not None:
        bias_cols, _ = load_bias_cols(P, epi["bias"], O // 128)
    scale_cols = None
    if epi.get("scale_cols") is not None:
        scale_cols, _ = load_bias_cols(P, epi["scale_cols"], O // 128, name="sc")
    colbuf = None
    if epi.get("colmul") is not None:
        crow = P.scratch("crow", [2, Tn], F32R)
        colbuf = P.scratch("colbuf", [128, Tn], F32)
        cms = P.sem("mm_util")
        cev = P.sem("ev_s")
        czv = P.zero2(crow[0:2, :], inc=cev)
        P.load("gpsimd", crow[0:1, :], epi["colmul"], waits=[(cev, czv)], sem="misc")
        clv = (P.dmasem("misc"), P.dmasem("misc").n)
        for tci in range(Tn // TT):
            cm = P.mm(P.bank(7), P.ones[0:2, :], crow[0:2, tci * TT:(tci + 1) * TT],
                      start=True, stop=True,
                      waits=[clv] + ([(cev, cev.n)] if tci > 0 else []), inc=cms)
            P.vop(lambda e, o=colbuf[:, tci * TT:(tci + 1) * TT], i=P.bank(7):
                  e.tensor_copy(o, i), waits=[(cms, cm)], inc=cev)
    wblk = P.scratch("wblk", [PD, 2 * nd * TT], F32R)
    xr = P.scratch("xr", [PD, 4 * TT], F32R)
    stg = P.scratch("stg", [128, 8 * TT], F32R)
    mms = P.sem("mm_s")
    evs = P.sem("ev_s")
    sp_tmp = None
    sp_prev = [None] * 8
    if epi.get("softplus"):
        sp_tmp = P.scratch("sptmp", [128, 16 * TT], F32)
        sqs = P.sem("sq_s")
    func = epi.get("func", AF.Copy)
    scale = epi.get("scale", 1.0)
    gate_fn = epi.get("gate_fn")
    wld = []
    wfree = [None, None]
    xslot_free = [None] * 4
    stg_free = [None] * 8
    bank_free = [None, None]
    if colbuf is not None:
        bank_free[1] = evs.n
    xcount = 0
    for og in range(nog):
        wv = [] if wfree[og % 2] is None else [(mms, wfree[og % 2])]
        last = None
        for dc in range(nd):
            last = P.load("gpsimd",
                          wblk[:, (og % 2) * nd * TT + dc * TT:(og % 2) * nd * TT + (dc + 1) * TT],
                          w_dram[dc * PD:(dc + 1) * PD, og * TT:(og + 1) * TT],
                          waits=wv if dc == 0 else (), sem=f"w{og % 2}")
        wld.append(last)
        for t in range(ntt):
            tp = t % 2
            lastmm = None
            for dc in range(nd):
                slot = xcount % 4
                xw = [] if xslot_free[slot] is None else [(mms, xslot_free[slot])]
                lv = P.load(("sync", "scalar")[xcount % 2],
                            xr[:, slot * TT:(slot + 1) * TT],
                            x_dram[dc * PD:(dc + 1) * PD, t * TT:(t + 1) * TT],
                            waits=xw, sem=f"x{slot}")
                xcount += 1
                mmw = [lv, wld[og]]
                if dc == 0 and bank_free[tp] is not None:
                    mmw.append((evs, bank_free[tp]))
                for oc in range(4):
                    lastmm = P.mm(P.bank(tp * 4 + oc),
                                  wblk[:, (og % 2) * nd * TT + dc * TT + oc * 128:
                                       (og % 2) * nd * TT + dc * TT + oc * 128 + 128],
                                  xr[:, slot * TT:(slot + 1) * TT],
                                  start=(dc == 0), stop=(dc == nd - 1),
                                  waits=mmw if oc == 0 else (),
                                  inc=mms if oc == 3 else None)
                xslot_free[slot] = lastmm
            wfree[og % 2] = mms.n
            evlist = []
            for oc in range(4):
                orow = og * TT + oc * 128
                sslot = tp * 4 + oc
                sa = stg[:, sslot * TT:(sslot + 1) * TT]
                wv = [(mms, lastmm)]
                if stg_free[sslot] is not None:
                    wv.append(stg_free[sslot])
                bank_ap = P.bank(tp * 4 + oc)
                if sp_tmp is not None:
                    bcol = bias_cols[:, orow // 128:orow // 128 + 1]
                    t0 = sp_tmp[:, sslot * 2 * TT: sslot * 2 * TT + TT]
                    t1 = sp_tmp[:, sslot * 2 * TT + TT: sslot * 2 * TT + 2 * TT]
                    spw = list(wv)
                    if sp_prev[sslot] is not None:
                        spw.append((evs, sp_prev[sslot]))
                    P.act(t0, bank_ap, AF.Abs, bias=bcol, waits=spw)
                    P.act(t1, t0, AF.Exp, scale=-1.0)
                    P.act(t1, t1, AF.Ln, bias=1.0)
                    s4 = P.act(t0, bank_ap, AF.Relu, bias=bcol, inc=sqs)
                    stw = [(sqs, s4)]
                    if stg_free[sslot] is not None:
                        stw.append(stg_free[sslot])
                    ev = P.vop(lambda e, o=sa, x=t0, y=t1:
                               e.tensor_tensor(o, x, y, ALU.add), waits=stw, inc=evs)
                    sp_prev[sslot] = ev
                elif gate_fn is not None:
                    ev = gate_fn(P, bank_ap, sa, orow, t, wv, evs)
                elif colbuf is not None:
                    ev = P.vop(lambda e, o=sa, x=bank_ap,
                               y=colbuf[:, t * TT:(t + 1) * TT]:
                               e.tensor_tensor(o, x, y, ALU.mult), waits=wv, inc=evs)
                elif bias_cols is not None and func == AF.Copy and scale == 1.0:
                    ev = P.vop(lambda e, o=sa, i=bank_ap,
                               b=bias_cols[:, orow // 128:orow // 128 + 1]:
                               e.tensor_scalar(o, i, b, None, ALU.add),
                               waits=wv, inc=evs)
                elif bias_cols is not None:
                    ev = P.act(sa, bank_ap, func,
                               bias=bias_cols[:, orow // 128:orow // 128 + 1],
                               scale=scale, waits=wv, inc=evs)
                elif scale_cols is not None:
                    ev = P.act(sa, bank_ap, func,
                               scale=scale_cols[:, orow // 128:orow // 128 + 1],
                               waits=wv, inc=evs)
                elif func == AF.Copy and scale == 1.0:
                    ev = P.vop(lambda e, o=sa, i=bank_ap: e.tensor_copy(o, i),
                               waits=wv, inc=evs)
                else:
                    ev = P.act(sa, bank_ap, func, scale=scale, waits=wv, inc=evs)
                evlist.append((ev, sa, orow, sslot))
            bank_free[tp] = evs.n
            for (ev, sa, orow, sslot) in evlist:
                stg_free[sslot] = P.store(
                    "gpsimd" if accum_out else "sync",
                    out_dram[out_row0 + orow:out_row0 + orow + 128, t * TT:(t + 1) * TT],
                    sa, waits=[(evs, ev)], accum=accum_out, sem=f"so{sslot}")
    P.end_stage()



def emit_projT(P, name, w_dram, x_dram, out_dram, D, O, epi=None, Tn=T):
    if epi is None:
        epi = {}
    nd = D // 128
    nog = O // TT
    ntt = Tn // TT
    P.begin_stage(name)
    wblk = P.scratch("wblk", [128, 2 * nd * TT], F32R)
    xblk = P.scratch("xblk", [128, 2 * nd * TT], F32R)
    stg = P.scratch("stg", [128, 8 * TT], F32R)
    mms = P.sem("mm_s")
    evs = P.sem("ev_s")
    func = epi.get("func", AF.Copy)
    scale = epi.get("scale", 1.0)
    post = epi.get("post")
    wfree = [None, None]
    xfree = [None, None]
    stg_free = [None] * 8
    bank_free = [None, None]
    for og in range(nog):
        wv = [] if wfree[og % 2] is None else [(mms, wfree[og % 2])]
        lastw = None
        for dc in range(nd):
            lastw = P.load("gpsimd",
                           wblk[:, (og % 2) * nd * TT + dc * TT:(og % 2) * nd * TT + (dc + 1) * TT],
                           w_dram[dc * 128:(dc + 1) * 128, og * TT:(og + 1) * TT],
                           waits=wv if dc == 0 else (), sem=f"w{og % 2}")
        for t4 in range(ntt):
            xb = t4 % 2
            xv = [] if xfree[xb] is None else [(mms, xfree[xb])]
            lastx = {}
            for dc in range(nd):
                qn = ("sync", "scalar")[dc % 2]
                lastx[qn] = P.load(qn,
                                   xblk[:, xb * nd * TT + dc * TT: xb * nd * TT + (dc + 1) * TT],
                                   x_dram[dc * 128:(dc + 1) * 128, t4 * TT:(t4 + 1) * TT],
                                   waits=xv if dc < 2 else (), sem=f"x{(t4 % 2) * 8 + dc % 8}")
            tp = t4 % 2
            lastmm = {}
            for tc in range(4):
                mmw = [lastw] + list(lastx.values())
                if bank_free[tp] is not None:
                    mmw.append((evs, bank_free[tp]))
                for dc in range(nd):
                    lastmm[tc] = P.mm(P.bank(tp * 4 + tc),
                                      xblk[:, xb * nd * TT + dc * TT + tc * 128:
                                           xb * nd * TT + dc * TT + tc * 128 + 128],
                                      wblk[:, (og % 2) * nd * TT + dc * TT:
                                           (og % 2) * nd * TT + (dc + 1) * TT],
                                      start=(dc == 0), stop=(dc == nd - 1),
                                      waits=mmw if dc == 0 else (),
                                      inc=mms if dc == nd - 1 else None)
            xfree[xb] = mms.n
            wfree[og % 2] = mms.n
            evlist = []
            for tc in range(4):
                sslot = tp * 4 + tc
                sa = stg[:, sslot * TT:(sslot + 1) * TT]
                trow = t4 * TT + tc * 128
                wv = [(mms, lastmm[tc])]
                if stg_free[sslot] is not None:
                    wv.append(stg_free[sslot])
                if post is not None:
                    ev = post(P, P.bank(sslot), sa, trow, wv, evs)
                elif func == AF.Copy and scale == 1.0:
                    ev = P.vop(lambda e, o=sa, i=P.bank(sslot): e.tensor_copy(o, i),
                               waits=wv, inc=evs)
                else:
                    ev = P.act(sa, P.bank(sslot), func, scale=scale, waits=wv, inc=evs)
                evlist.append((ev, sa, trow, sslot))
            bank_free[tp] = evs.n
            for (ev, sa, trow, sslot) in evlist:
                stg_free[sslot] = P.store("sync",
                                          out_dram[trow:trow + 128, og * TT:(og + 1) * TT],
                                          sa, waits=[(evs, ev)], sem=f"so{sslot}")
    P.end_stage()



def emit_ew(P, name, ins, out_dram, body, D, Tn=T, row0s=None):
    if row0s is None:
        row0s = [0] * len(ins)
    nd = D // 128
    ntt = Tn // TT
    P.begin_stage(name)
    k = len(ins)
    tiles = P.scratch("in", [128, 2 * k * TT], F32)
    ot = P.scratch("out", [128, 2 * TT], F32R)
    evs = P.sem("ev_s")
    prev_ev = [None, None]
    prev_st = [None, None]
    idx = 0
    for t in range(ntt):
        for dc in range(nd):
            b = idx % 2
            ldv = []
            for i in range(k):
                wv = [] if prev_ev[b] is None else [(evs, prev_ev[b])]
                ldv.append(P.load(("sync", "scalar")[i % 2],
                                  tiles[:, (b * k + i) * TT:(b * k + i + 1) * TT],
                                  ins[i][row0s[i] + dc * 128:row0s[i] + (dc + 1) * 128,
                                         t * TT:(t + 1) * TT].bitcast(F32),
                                  waits=wv, sem=f"x{b * k + i}"))
            wv = list(ldv)
            if prev_st[b] is not None:
                wv.append(prev_st[b])
            ev = body(P, [tiles[:, (b * k + i) * TT:(b * k + i + 1) * TT] for i in range(k)],
                      ot[:, b * TT:(b + 1) * TT], wv, evs, dc, t)
            prev_ev[b] = ev
            prev_st[b] = P.store("sync", out_dram[dc * 128:(dc + 1) * 128, t * TT:(t + 1) * TT],
                                 ot[:, b * TT:(b + 1) * TT], waits=[(evs, ev)], sem=f"so{b}")
            idx += 1
    P.end_stage()



L0_H = 4


def emit_rope_apply(P, name, src_dram, dst_dram, scale=1.0):
    P.begin_stage(name)
    cosf = P.scratch("cos", [128, T], F32)
    sinf = P.scratch("sin", [128, T], F32)
    ph = P.scratch("ph", [128, T], F32)
    tmp = P.scratch("tmp", [128, T], F32)
    ti = P.scratch("ti", [128, T], I32)
    qcol = P.scratch("qcol", [128, 1], F32)
    iv = P.scratch("iv", [128, 1], F32)
    dr = P.sem("gp_done")
    P.vop(lambda e: e.iota(qcol[:], [[0, 1]], base=0, channel_multiplier=1,
                           allow_small_or_imprecise_dtypes=True), eng="gpsimd")
    P.vop(lambda e: e.iota(ph[:], [[1, T]], base=0, channel_multiplier=0,
                           allow_small_or_imprecise_dtypes=True), eng="gpsimd")
    P.rec("gpsimd", lambda e: e.drain())
    P.rec("gpsimd", lambda e: e.nop(), inc=dr)
    P.pending["vector"].append((dr, dr.n))
    P.pending["scalar"].append((dr, dr.n))
    P.act(iv[:], qcol[:], AF.Exp, scale=-np.log(10000.0) / 128.0)
    scs = P.sem("sq_s")
    sv0 = P.rec("scalar", lambda e: e.nop(), inc=scs)
    P.vop(lambda e: e.tensor_scalar_mul(ph[:], ph[:], iv[:, 0:1]), waits=[(scs, sv0)])

    def build_trig(dst, shift):
        twopi = float(2 * np.pi)
        P.vop(lambda e, o=tmp[:]: e.tensor_scalar(o, ph[:], 1.0 / twopi, shift / twopi,
                                                  ALU.mult, ALU.add))
        P.vop(lambda e: e.tensor_copy(ti[:], tmp[:]))
        P.vop(lambda e, o=dst: e.tensor_copy(o, ti[:]))
        P.vop(lambda e, o=dst: e.tensor_tensor(o, tmp[:], o, ALU.subtract))
        P.vop(lambda e, o=dst: e.tensor_scalar_mul(o, o, twopi))
        P.vop(lambda e, o=tmp[:], d=dst: e.tensor_single_scalar(o, d, np.pi, ALU.is_gt))
        P.vop(lambda e, o=dst: e.scalar_tensor_tensor(o, tmp[:], -twopi, o, ALU.mult, ALU.add))
        P.vop(lambda e, o=tmp[:], d=dst: e.tensor_single_scalar(o, d, -np.pi, ALU.is_lt))
        vv = P.vop(lambda e, o=dst: e.scalar_tensor_tensor(o, tmp[:], twopi, o, ALU.mult, ALU.add),
                   inc=P.sem("ev_s"))
        P.act(dst, dst, AF.Sin, waits=[(P.sem("ev_s"), vv)])

    build_trig(sinf[:], 0.0)
    build_trig(cosf[:], np.pi / 2)
    scv = P.rec("scalar", lambda e: e.drain())
    scv = P.rec("scalar", lambda e: e.nop(), inc=scs)
    P.pending["vector"].append((scs, scs.n))

    xa = P.scratch("xa", [128, 2 * TT], F32)
    xb = P.scratch("xb", [128, 2 * TT], F32)
    ya = P.scratch("ya", [128, 2 * TT], F32R)
    yb = P.scratch("yb", [128, 2 * TT], F32R)
    tm2 = P.scratch("tm2", [128, 2 * TT], F32)
    evs = P.sem("ev_s")
    slot_ev = [None, None]
    slot_st = [None, None]
    for t in range(NTT):
        for h in range(L0_H):
            b = (t * L0_H + h) % 2
            ca = cosf[:, t * TT:(t + 1) * TT]
            sa = sinf[:, t * TT:(t + 1) * TT]
            wv = [] if slot_ev[b] is None else [(evs, slot_ev[b])]
            l1 = P.load("sync", xa[:, b * TT:(b + 1) * TT],
                        src_dram[h * 256:h * 256 + 128, t * TT:(t + 1) * TT].bitcast(F32),
                        waits=wv, sem=f"x{b}")
            l2 = P.load("scalar", xb[:, b * TT:(b + 1) * TT],
                        src_dram[h * 256 + 128:h * 256 + 256, t * TT:(t + 1) * TT].bitcast(F32),
                        waits=wv, sem=f"x{2 + b}")
            a1 = xa[:, b * TT:(b + 1) * TT]
            a2 = xb[:, b * TT:(b + 1) * TT]
            o1 = ya[:, b * TT:(b + 1) * TT]
            o2 = yb[:, b * TT:(b + 1) * TT]
            tmv = tm2[:, b * TT:(b + 1) * TT]
            wv2 = [l1, l2]
            if slot_st[b] is not None:
                wv2 = wv2 + slot_st[b]
            P.vop(lambda e, o=o1, a=a1, c=ca: e.tensor_tensor(o, a, c, ALU.mult), waits=wv2)
            P.vop(lambda e, o=tmv, a=a2, c=sa: e.tensor_tensor(o, a, c, ALU.mult))
            if scale != 1.0:
                P.vop(lambda e, o=o1, a=o1.bitcast(F32), bb=tmv:
                      e.tensor_tensor(o, a, bb, ALU.subtract))
                P.vop(lambda e, o=o1, i=o1.bitcast(F32): e.tensor_scalar_mul(o, i, scale))
            else:
                P.vop(lambda e, o=o1, a=o1.bitcast(F32), bb=tmv:
                      e.tensor_tensor(o, a, bb, ALU.subtract))
            P.vop(lambda e, o=o2, a=a1, c=sa: e.tensor_tensor(o, a, c, ALU.mult))
            P.vop(lambda e, o=tmv, a=a2, c=ca: e.tensor_tensor(o, a, c, ALU.mult))
            if scale != 1.0:
                P.vop(lambda e, o=o2, a=o2.bitcast(F32), bb=tmv:
                      e.tensor_tensor(o, a, bb, ALU.add))
                ev = P.vop(lambda e, o=o2, i=o2.bitcast(F32): e.tensor_scalar_mul(o, i, scale),
                           inc=evs)
            else:
                ev = P.vop(lambda e, o=o2, a=o2.bitcast(F32), bb=tmv:
                           e.tensor_tensor(o, a, bb, ALU.add), inc=evs)
            s1 = P.store("sync", dst_dram[h * 256:h * 256 + 128, t * TT:(t + 1) * TT], o1,
                         waits=[(evs, ev)], sem=f"so{b}")
            s2 = P.store("scalar", dst_dram[h * 256 + 128:h * 256 + 256, t * TT:(t + 1) * TT], o2,
                         waits=[(evs, ev)], sem=f"so{2 + b}")
            slot_ev[b] = ev
            slot_st[b] = [s1, s2]
    P.end_stage()


def emit_l0_retention(P, qr_dram, kr_dram, vt_dram, o_dram):
    C = TT
    P.begin_stage("l0ret")
    lam = [float(np.log1p(-np.exp2(-5.0 - h))) for h in range(L0_H)]
    ident = make_identity(P)
    qrb = P.scratch("qrb", [128, 2 * TT], F32R)
    krb = P.scratch("krb", [128, 2 * TT], F32R)
    qt_ = P.scratch("qt", [128, 2 * TT], F32R)
    kfm = P.scratch("kfm", [128, 2 * TT], F32)
    ktt = P.scratch("ktt", [128, 4 * 256], F32R)
    vtb = P.scratch("vtb", [128, 4 * TT], F32R)
    attn = P.scratch("attn", [128, 4 * TT], F32R)
    stg = P.scratch("stg", [128, 4 * TT], F32R)
    S = P.scratch("S", [128, 8 * TT], F32R)
    masks = P.scratch("masks", [128, 16 * TT], F32)
    Fh = P.scratch("Fh", [128, 4 * TT], F32)
    F2 = P.scratch("F2", [128, 4 * TT], F32)
    iot = P.scratch("iot", [128, TT], F32)
    icol = P.scratch("icol", [128, 1], F32)
    mtmp = P.scratch("mtmp", [128, 4 * TT], F32)
    mms = P.sem("mm_s")
    evs = P.sem("ev_s")
    dr = P.sem("gp_done")
    P.vop(lambda e: e.iota(iot[:], [[1, TT]], base=0, channel_multiplier=0,
                           allow_small_or_imprecise_dtypes=True), eng="gpsimd")
    P.vop(lambda e: e.iota(icol[:], [[0, 1]], base=0, channel_multiplier=1,
                           allow_small_or_imprecise_dtypes=True), eng="gpsimd")
    P.rec("gpsimd", lambda e: e.drain())
    P.rec("gpsimd", lambda e: e.nop(), inc=dr)
    P.pending["vector"].append((dr, dr.n))
    P.pending["scalar"].append((dr, dr.n))
    tbs = P.sem("sq_s")
    for h in range(L0_H):
        for so in range(4):
            ma = masks[:, (h * 4 + so) * TT:(h * 4 + so + 1) * TT]
            mt = masks[:, (h * 4 + so) * TT:(h * 4 + so) * TT + TT]
            P.vop(lambda e, o=ma, i=iot[:], sh=-so * 128.0: e.tensor_scalar(o, i, 1.0, sh,
                                                                           ALU.mult, ALU.add))
            P.vop(lambda e, o=ma: e.tensor_scalar(o, o, icol[:, 0:1], None, ALU.subtract))
            mo = (h * 4 + so) % 4
            mtm = mtmp[:, mo * TT:(mo + 1) * TT]
            P.vop(lambda e, o=mtm, i=ma: e.tensor_single_scalar(o, i, 0.0, ALU.is_ge))
            v1 = P.vop(lambda e, o=ma, lh=85.0 / lam[h]:
                       e.tensor_single_scalar(o, o, lh, ALU.max), inc=evs)
            s1 = P.act(ma, ma, AF.Exp, scale=lam[h], waits=[(evs, v1)], inc=tbs)
            P.vop(lambda e, o=ma, a=ma, b=mtm: e.tensor_tensor(o, a, b, ALU.mult),
                  waits=[(tbs, s1)], inc=evs)
        fa = Fh[:, h * TT:(h + 1) * TT]
        P.act(fa, iot[:], AF.Exp, scale=lam[h], bias=lam[h], inc=tbs)
        f2 = F2[:, h * TT:(h + 1) * TT]
        v2 = P.vop(lambda e, o=f2, i=iot[:]: e.tensor_scalar(o, i, -1.0, float(C - 1),
                                                             ALU.mult, ALU.add), inc=evs)
        P.act(f2, f2, AF.Exp, scale=lam[h], waits=[(evs, v2)], inc=tbs)
    P.fence(("vector", "scalar"))
    P.vop(lambda e: e.memset(kfm[:, 0:TT], 0.0))
    for i in range(8):
        P.vop(lambda e, o=S[:, i * TT:(i + 1) * TT]: e.tensor_copy(o, kfm[:, 0:TT]))
    zs = P.vop(lambda e: e.engine_nop(), inc=evs)
    P.pending["tensor"].append((evs, zs))
    stg_st = [None] * 4
    prev_mark = None
    for c in range(NTT):
        for h in range(L0_H):
            pw = [] if prev_mark is None else [(mms, prev_mark[0]), (evs, prev_mark[1])]
            lq, lk, lvv = [], [], []
            for dc in range(2):
                lq.append(P.load("sync", qrb[:, dc * TT:(dc + 1) * TT],
                                 qr_dram[h * 256 + dc * 128:h * 256 + (dc + 1) * 128,
                                         c * TT:(c + 1) * TT], waits=pw if dc == 0 else (),
                                 sem=f"x{dc}"))
                lk.append(P.load("scalar", krb[:, dc * TT:(dc + 1) * TT],
                                 kr_dram[h * 256 + dc * 128:h * 256 + (dc + 1) * 128,
                                         c * TT:(c + 1) * TT], waits=pw if dc == 0 else (),
                                 sem=f"x{2 + dc}"))
            for tc in range(4):
                lvv.append(P.load("gpsimd", vtb[:, tc * TT:(tc + 1) * TT],
                                  vt_dram[c * TT + tc * 128:c * TT + (tc + 1) * 128,
                                          h * 512:(h + 1) * 512], waits=pw if tc == 0 else (),
                                  sem=f"x{4 + tc}"))
            qv, kv = [], []
            for dc in range(2):
                qv.append(P.vop(lambda e, o=qt_[:, dc * TT:(dc + 1) * TT],
                                a=qrb[:, dc * TT:(dc + 1) * TT].bitcast(F32),
                                f=Fh[:, h * TT:(h + 1) * TT]:
                                e.tensor_tensor(o, a, f, ALU.mult),
                                waits=[lq[dc]], inc=evs))
                kv.append(P.vop(lambda e, o=kfm[:, dc * TT:(dc + 1) * TT],
                                a=krb[:, dc * TT:(dc + 1) * TT].bitcast(F32),
                                f=F2[:, h * TT:(h + 1) * TT]:
                                e.tensor_tensor(o, a, f, ALU.mult),
                                waits=[lk[dc]], inc=evs))
            sv = {}
            for sc in range(4):
                for dc in range(2):
                    sv[sc] = P.mm(P.bank(sc),
                                  krb[:, dc * TT + sc * 128:dc * TT + sc * 128 + 128],
                                  qrb[:, dc * TT:(dc + 1) * TT],
                                  start=(dc == 0), stop=(dc == 1),
                                  waits=[lq[0], lq[1], lk[0], lk[1]] if (sc == 0 and dc == 0) else (),
                                  inc=mms if dc == 1 else None)
            av = {}
            for sc in range(4):
                av[sc] = P.vop(lambda e, o=attn[:, sc * TT:(sc + 1) * TT], a=P.bank(sc),
                               m=masks[:, (h * 4 + sc) * TT:(h * 4 + sc + 1) * TT]:
                               e.tensor_tensor(o, a, m, ALU.mult),
                               waits=[(mms, sv[sc])], inc=evs)
            ov = {}
            for dvc in range(4):
                for sc in range(4):
                    P.mm(P.bank(4 + dvc),
                         vtb[:, sc * TT + dvc * 128:sc * TT + dvc * 128 + 128],
                         attn[:, sc * TT:(sc + 1) * TT],
                         start=(sc == 0), stop=False,
                         waits=[lvv[sc], (evs, av[sc])])
                for dc in range(2):
                    ov[dvc] = P.mm(P.bank(4 + dvc),
                                   S[:, (h * 2 + dc) * TT + dvc * 128:
                                     (h * 2 + dc) * TT + dvc * 128 + 128],
                                   qt_[:, dc * TT:(dc + 1) * TT],
                                   start=False, stop=(dc == 1),
                                   waits=[(evs, qv[dc])],
                                   inc=mms if dc == 1 else None)
            trv = {}
            for tc in range(4):
                bk = P.bank(tc % 2, n=128)
                for dc in range(2):
                    wv = []
                    if tc < 2 and dc == 0:
                        wv = [(evs, av[0]), (evs, av[1]), (evs, kv[0]), (evs, kv[1])]
                    elif dc == 0:
                        wv = [(evs, trv[(tc - 2, 1)])]
                    else:
                        wv = [(evs, trv[(tc, 0)])]
                    m = P.trans(bk, kfm[:, dc * TT + tc * 128:dc * TT + tc * 128 + 128],
                                ident[:], waits=wv, inc=mms)
                    trv[(tc, dc)] = P.vop(
                        lambda e, o=ktt[:, tc * 256 + dc * 128:tc * 256 + (dc + 1) * 128],
                        i=bk: e.tensor_copy(o, i), waits=[(mms, m)], inc=evs)
            sadd = {}
            for dc in range(2):
                for tc in range(4):
                    wv = [(evs, trv[(tc, dc)])]
                    if tc == 0:
                        wv.append((evs, av[2]))
                        wv.append((evs, av[3]))
                    sadd[dc] = P.mm(P.bank(2 + dc),
                                    ktt[:, tc * 256 + dc * 128:tc * 256 + (dc + 1) * 128],
                                    vtb[:, tc * TT:(tc + 1) * TT],
                                    start=(tc == 0), stop=(tc == 3),
                                    waits=wv, inc=mms if tc == 3 else None)
            lamC = float(np.exp(lam[h] * C))
            for dc in range(2):
                P.vop(lambda e, o=S[:, (h * 2 + dc) * TT:(h * 2 + dc + 1) * TT],
                      a=P.bank(2 + dc), lc=lamC:
                      e.scalar_tensor_tensor(o, o.bitcast(F32), lc, a, ALU.mult, ALU.add),
                      waits=[(mms, sadd[dc]), (mms, ov[3])], inc=evs)
            for dvc in range(4):
                wv = [(mms, ov[dvc])]
                if stg_st[dvc] is not None:
                    wv.append(stg_st[dvc])
                ev = P.vop(lambda e, o=stg[:, dvc * TT:(dvc + 1) * TT], i=P.bank(4 + dvc):
                           e.tensor_copy(o, i), waits=wv, inc=evs)
                stg_st[dvc] = P.store("sync",
                                      o_dram[h * 512 + dvc * 128:h * 512 + (dvc + 1) * 128,
                                             c * TT:(c + 1) * TT],
                                      stg[:, dvc * TT:(dvc + 1) * TT], waits=[(evs, ev)],
                                      sem=f"so{dvc}")
            P.pending["tensor"].append((evs, evs.n))
            prev_mark = (mms.n, evs.n)
    P.end_stage()


def emit_l0_rmsgate(P, o_dram, sg_dram, gnw_dram, og_dram):
    eps = 1e-5
    P.begin_stage("l0rms")
    gcols, _ = load_bias_cols(P, gnw_dram, 4, "gnw")
    ob = P.scratch("ob", [128, 4 * TT], F32R)
    o2 = P.scratch("o2", [128, 4 * TT], F32R)
    sgb = P.scratch("sgb", [128, 4 * TT], F32)
    rows = P.scratch("rows", [1, TT], F32)
    rowr = P.scratch("rowr", [2, TT], F32R)
    stg = P.scratch("stg", [128, 4 * TT], F32R)
    mms = P.sem("mm_s")
    evs = P.sem("ev_s")
    sqs = P.sem("sq_s")
    P.zero2(rowr[0:2, :])
    stg_st = [None] * 4
    last_bm = None
    for t in range(NTT):
        for h in range(L0_H):
            lo, ls = [], []
            for dvc in range(4):
                lo.append(P.load("sync", ob[:, dvc * TT:(dvc + 1) * TT],
                                 o_dram[h * 512 + dvc * 128:h * 512 + (dvc + 1) * 128,
                                        t * TT:(t + 1) * TT],
                                 waits=[(evs, evs.n)] if dvc == 0 else (), sem=f"x{dvc}"))
                ls.append(P.load("scalar", sgb[:, dvc * TT:(dvc + 1) * TT],
                                 sg_dram[h * 512 + dvc * 128:h * 512 + (dvc + 1) * 128,
                                         t * TT:(t + 1) * TT].bitcast(F32),
                                 waits=[(evs, evs.n)] if dvc == 0 else (), sem=f"x{4 + dvc}"))
            sq = []
            for dvc in range(4):
                sq.append(P.act(o2[:, dvc * TT:(dvc + 1) * TT], ob[:, dvc * TT:(dvc + 1) * TT],
                                AF.Square, waits=[lo[dvc]], inc=sqs))
            mv = None
            for dvc in range(4):
                mv = P.mm(P.bank(0)[0:1, :], P.ones[:, 0:1], o2[:, dvc * TT:(dvc + 1) * TT],
                          start=(dvc == 0), stop=(dvc == 3),
                          waits=[(sqs, sq[dvc]), (evs, evs.n)] if dvc == 0 else [(sqs, sq[dvc])],
                          inc=mms if dvc == 3 else None)
            P.vop(lambda e, o=rows[0:1, :], i=P.bank(0)[0:1, :]:
                  e.tensor_scalar(o, i, 1.0 / 512, eps, ALU.mult, ALU.add), waits=[(mms, mv)])
            rcv = P.vop(lambda e, o=rows[0:1, :]: e.reciprocal(o, o), inc=evs)
            rv = P.act(rowr[0:1, :], rows[0:1, :], AF.Sqrt,
                       waits=[(evs, rcv)] + ([] if last_bm is None else [(mms, last_bm)]),
                       inc=sqs)
            bm = P.mm(P.bank(1), P.ones[0:2, :], rowr[0:2, :], start=True, stop=True,
                      waits=[(sqs, rv)], inc=mms)
            last_bm = bm
            for dvc in range(4):
                oa = ob[:, dvc * TT:(dvc + 1) * TT].bitcast(F32)
                sa = stg[:, dvc * TT:(dvc + 1) * TT]
                wv = [(mms, bm)]
                if stg_st[dvc] is not None:
                    wv.append(stg_st[dvc])
                P.vop(lambda e, o=sa, a=oa, b=P.bank(1): e.tensor_tensor(o, a, b, ALU.mult),
                      waits=wv)
                P.vop(lambda e, o=sa, i=sa.bitcast(F32), g=gcols[:, dvc:dvc + 1]:
                      e.tensor_scalar(o, i, g, None, ALU.mult))
                ev = P.vop(lambda e, o=sa, a=sa.bitcast(F32), b=sgb[:, dvc * TT:(dvc + 1) * TT]:
                           e.tensor_tensor(o, a, b, ALU.mult), waits=[ls[dvc]], inc=evs)
                stg_st[dvc] = P.store("sync",
                                      og_dram[h * 512 + dvc * 128:h * 512 + (dvc + 1) * 128,
                                              t * TT:(t + 1) * TT], sa, waits=[(evs, ev)],
                                      sem=f"so{dvc}")
    P.end_stage()


def emit_xupdate(P, name, x_dram, s_dram, g_dram, out_dram):
    def body(P, tiles, out, wv, evs, dc, t):
        xs, ss, gs = tiles
        P.vop(lambda e, o=out, a=ss, b=gs: e.tensor_tensor(o, a, b, ALU.mult), waits=wv)
        return P.vop(lambda e, o=out, a=out.bitcast(F32), b=xs:
                     e.tensor_tensor(o, a, b, ALU.add), inc=evs)
    emit_ew(P, name, [x_dram, s_dram, g_dram], out_dram, body, DM)


def emit_layer0(P, x_in, x_out):
    xn = P.dram("l0_xn", [DM, T], F32R)
    emit_layernorm(P, "l0ln", x_in, xn)
    q0 = P.dram("l0_q", [DM, T], F32R)
    k0 = P.dram("l0_k", [DM, T], F32R)
    qr = P.dram("l0_qr", [DM, T], F32R)
    kr = P.dram("l0_kr", [DM, T], F32R)
    vt = P.dram("l0_vt", [T, 2048], F32R)
    sg = P.dram("l0_sg", [2048, T], F32R)
    o0 = P.dram("l0_o", [2048, T], F32R)
    og = P.dram("l0_og", [2048, T], F32R)
    s0 = P.dram("l0_s", [DM, T], F32R)
    emit_proj(P, "l0q", P.dram("L0_attn_Wq", [DM, DM], F32R, kind="ExternalInput"),
              xn, q0, DM, DM)
    emit_proj(P, "l0k", P.dram("L0_attn_Wk", [DM, DM], F32R, kind="ExternalInput"),
              xn, k0, DM, DM)
    emit_projT(P, "l0v", P.dram("L0_attn_Wv", [DM, 2048], F32R, kind="ExternalInput"),
               xn, vt, DM, 2048)
    emit_proj(P, "l0g", P.dram("L0_attn_Wg", [DM, 2048], F32R, kind="ExternalInput"),
              xn, sg, DM, 2048, epi={"func": AF.Silu})
    emit_rope_apply(P, "l0ropeq", q0, qr, scale=256.0 ** -0.5)
    emit_rope_apply(P, "l0ropek", k0, kr)
    emit_l0_retention(P, qr, kr, vt, o0)
    emit_l0_rmsgate(P, o0, sg,
                    P.dram("L0_attn_g_norm_w", [512], F32R, kind="ExternalInput"), og)
    emit_proj(P, "l0wo", P.dram("L0_attn_Wo", [2048, DM], F32R, kind="ExternalInput"),
              og, s0, 2048, DM)
    gate = P.dram("l0_gate", [3072, T], F32R)
    up = P.dram("l0_up", [3072, T], F32R)
    h0 = P.dram("l0_h", [3072, T], F32R)
    g0 = P.dram("l0_g", [DM, T], F32R)
    emit_proj(P, "l0fg", P.dram("L0_ffn_w_gate", [DM, 3072], I32, kind="ExternalInput"),
              xn, gate, DM, 3072, epi={"func": AF.Silu, "scale": 1.0 / 127.0})
    emit_proj(P, "l0fu", P.dram("L0_ffn_w_up", [DM, 3072], I32, kind="ExternalInput"),
              xn, up, DM, 3072, epi={"func": AF.Copy, "scale": 1.0 / 127.0})

    def hbody(P, tiles, out, wv, evs, dc, t):
        a, b = tiles
        return P.vop(lambda e, o=out, x=a, y=b: e.tensor_tensor(o, x, y, ALU.mult),
                     waits=wv, inc=evs)
    emit_ew(P, "l0h", [gate, up], h0, hbody, 3072)
    emit_proj(P, "l0fd", P.dram("L0_ffn_w_down", [3072, DM], I32, kind="ExternalInput"),
              h0, g0, 3072, DM, epi={"func": AF.Silu, "scale": 1.0 / 127.0})
    emit_xupdate(P, "l0xu", x_in, s0, g0, x_out)



L1_H = 8
FP4_THRESH = [0.25, 0.75, 1.25, 1.75, 2.5, 2.5, 3.5, 3.5, 5.0, 5.0, 5.0, 5.0]


def _fp4_quant_tile(P, tmw, aq, lvl, lvl2, scr, n, evs, sqs, wv):
    nb = n // 16
    a3 = aq[:, 0:n].rearrange("p (b c) -> p b c", c=16)
    s1 = P.act(aq[:, 0:n], tmw, AF.Abs, waits=wv, inc=sqs)
    r8 = scr[:, 0:8 * nb].rearrange("p (b c) -> p b c", c=8)
    r4 = scr[:, 8 * nb:12 * nb].rearrange("p (b c) -> p b c", c=4)
    r2 = scr[:, 12 * nb:14 * nb].rearrange("p (b c) -> p b c", c=2)
    bmax = scr[:, 14 * nb:15 * nb]
    sc1 = scr[:, 15 * nb:16 * nb]
    sc2 = scr[:, 16 * nb:17 * nb]
    rsc = scr[:, 17 * nb:18 * nb]
    shalf = scr[:, 18 * nb:19 * nb]
    P.vop(lambda e: e.tensor_tensor(r8, a3[:, :, 0:8], a3[:, :, 8:16], ALU.max),
          waits=[(sqs, s1)])
    P.vop(lambda e: e.tensor_tensor(r4, r8[:, :, 0:4], r8[:, :, 4:8], ALU.max))
    P.vop(lambda e: e.tensor_tensor(r2, r4[:, :, 0:2], r4[:, :, 2:4], ALU.max))
    P.vop(lambda e: e.tensor_tensor(bmax.rearrange("p (b c) -> p b c", c=1),
                                    r2[:, :, 0:1], r2[:, :, 1:2], ALU.max))
    P.vop(lambda e: e.drain())
    P.vop(lambda e: e.tensor_scalar(sc1, bmax, 1.0 / 6.0, None, ALU.mult))
    P.vop(lambda e: e.tensor_single_scalar(sc2, sc1, 1e-12, ALU.max))
    P.vop(lambda e: e.drain())
    P.vop(lambda e: e.reciprocal(rsc, sc2))
    P.vop(lambda e: e.tensor_scalar(shalf, sc2, 0.5, None, ALU.mult))
    P.vop(lambda e: e.drain())
    rb = rsc.rearrange("p (b c) -> p b c", c=1).broadcast_to([128, nb, 16])
    P.vop(lambda e: e.tensor_tensor(lvl[:, 0:n].rearrange("p (b c) -> p b c", c=16),
                                    a3, rb, ALU.mult))
    P.vop(lambda e: e.drain())
    P.vop(lambda e: e.tensor_single_scalar(lvl2[:, 0:n], lvl[:, 0:n],
                                           FP4_THRESH[0], ALU.is_gt))
    bufs = [lvl2, aq]
    cur = 0
    lv_last = None
    for i_t, thr in enumerate(FP4_THRESH[1:]):
        lv_last = P.vop(lambda e, t=thr, i=bufs[cur][0:128, 0:n], o=bufs[1 - cur][0:128, 0:n]:
                        e.scalar_tensor_tensor(o, lvl[:, 0:n], t, i, ALU.is_gt, ALU.add),
                        inc=evs if i_t == len(FP4_THRESH) - 2 else None)
        cur = 1 - cur
    acc = bufs[cur]
    oth = bufs[1 - cur]
    s2 = P.act(lvl[:, 0:n], tmw, AF.Sign, waits=[(evs, lv_last)], inc=sqs)
    P.vop(lambda e: e.tensor_tensor(oth[0:128, 0:n], acc[0:128, 0:n], lvl[:, 0:n],
                                    ALU.mult), waits=[(sqs, s2)])
    sb = shalf.rearrange("p (b c) -> p b c", c=1).broadcast_to([128, nb, 16])
    P.vop(lambda e: e.drain())
    return P.vop(lambda e: e.tensor_tensor(tmw.rearrange("p (b c) -> p b c", c=16),
                                           oth[0:128, 0:n].rearrange("p (b c) -> p b c", c=16),
                                           sb, ALU.mult), inc=evs)


def emit_fp4_quant_w(P, name, worig_dram, wq_dram, O, D, Opad=None, Dpad=None, width=512):
    Opad = Opad or O
    Dpad = Dpad or D
    P.begin_stage(name)
    ident = make_identity(P)
    nw = D // width
    no = O // 128
    wt = P.scratch("wt", [128, 2 * width], F32)
    aq = P.scratch("aq", [128, width], F32)
    lvl = P.scratch("lvl", [128, width], F32)
    lvl2 = P.scratch("lvl2", [128, width], F32)
    scr = P.scratch("scr", [128, width + 8 * (width // 16)], F32)
    stg = P.scratch("stg", [128, 4 * 128], F32R)
    zstg = P.scratch("zstg", [128, 128], F32R)
    mms = P.sem("mm_s")
    evs = P.sem("ev_s")
    sqs = P.sem("sq_s")
    zv = P.vop(lambda e: e.memset(zstg[:].bitcast(F32), 0.0), inc=evs)
    for oc in range(O // 128, Opad // 128):
        for dr in range(Dpad // 128):
            P.store("sync", wq_dram[dr * 128:(dr + 1) * 128, oc * 128:(oc + 1) * 128],
                    zstg[:], waits=[(evs, zv)], sem="zpad")
    for dr in range(D // 128, Dpad // 128):
        for oc in range(O // 128):
            P.store("sync", wq_dram[dr * 128:(dr + 1) * 128, oc * 128:(oc + 1) * 128],
                    zstg[:], waits=[(evs, zv)], sem="zpad")
    prev_q = {}
    prev_st = {}
    prev_tr = {}
    for oc in range(no):
        for db in range(nw):
            b = (oc * nw + db) % 2
            lw = []
            if b in prev_q:
                lw.append((evs, prev_q[b]))
            if b in prev_tr:
                lw.append((mms, prev_tr[b]))
            lv = P.load("sync", wt[:, b * width:(b + 1) * width],
                        worig_dram[oc * 128:(oc + 1) * 128,
                                   db * width:(db + 1) * width].bitcast(F32),
                        waits=lw, sem=f"x{b}")
            qv = _fp4_quant_tile(P, wt[:, b * width:(b + 1) * width], aq, lvl, lvl2, scr,
                                 width, evs, sqs,
                                 [lv] + ([(evs, prev_q[1 - b])] if (1 - b) in prev_q else []))
            prev_q[b] = qv
            for j in range(width // 128):
                slot = j % 2
                bk = P.bank(slot, n=128)
                wv = [(evs, qv)]
                if slot in prev_st:
                    wv.append((evs, prev_st[slot][0]))
                m = P.trans(bk, wt[:, b * width + j * 128: b * width + (j + 1) * 128],
                            ident[:], waits=wv, inc=mms)
                prev_tr[b] = m
                ev = P.vop(lambda e, o=stg[:, slot * 128:(slot + 1) * 128],
                           i=bk: e.tensor_copy(o, i),
                           waits=[(mms, m)] + ([prev_st[slot][1]] if slot in prev_st else []),
                           inc=evs)
                st = P.store("scalar",
                             wq_dram[db * width + j * 128: db * width + (j + 1) * 128,
                                     oc * 128:(oc + 1) * 128],
                             stg[:, slot * 128:(slot + 1) * 128],
                             waits=[(evs, ev)], sem=f"so{slot}")
                prev_st[slot] = (ev, st)
    P.end_stage()


def emit_fp4_quant_act(P, name, src_dram, dst_dram, D, Tn):
    P.begin_stage(name)
    ident = make_identity(P)
    W = 512
    nw = D // W
    nt = Tn // 128
    fin = P.scratch("fin", [128, 2 * W], F32R)
    tmw = P.scratch("tmw", [128, 2 * W], F32)
    aq = P.scratch("aq", [128, W], F32)
    lvl = P.scratch("lvl", [128, W], F32)
    lvl2 = P.scratch("lvl2", [128, W], F32)
    scr = P.scratch("scr", [128, W + 8 * (W // 16)], F32)
    stg = P.scratch("stg", [128, 4 * 128], F32R)
    mms = P.sem("mm_s")
    evs = P.sem("ev_s")
    sqs = P.sem("sq_s")
    prev_q = {}
    prev_otr = {}
    out_ev = {}
    out_st = {}
    in_ev = {}
    for tc in range(nt):
        for db in range(nw):
            b = (tc * nw + db) % 2
            ldv = []
            for j in range(4):
                wv = []
                if j == 0 and b in prev_otr:
                    wv = [(mms, prev_otr[b])]
                ldv.append(P.load(("sync", "scalar")[j % 2],
                                  fin[:, b * W + j * 128: b * W + (j + 1) * 128],
                                  src_dram[db * W + j * 128: db * W + (j + 1) * 128,
                                           tc * 128:(tc + 1) * 128],
                                  waits=wv, sem=f"x{b * 4 + j}"))
            tin = {}
            for j in range(4):
                slot = j % 2
                bk = P.bank(slot, n=128)
                wv = [ldv[j]]
                if j >= 2:
                    wv.append((evs, tin[j - 2]))
                elif (b, slot) in in_ev:
                    wv.append((evs, in_ev[(b, slot)]))
                m = P.trans(bk, fin[:, b * W + j * 128: b * W + (j + 1) * 128].bitcast(F32),
                            ident[:], waits=wv, inc=mms)
                ew = [(mms, m)]
                if b in prev_q:
                    ew.append((evs, prev_q[b]))
                if b in prev_otr:
                    ew.append((mms, prev_otr[b]))
                tin[j] = P.vop(lambda e, o=tmw[:, b * W + j * 128: b * W + (j + 1) * 128],
                               i=bk: e.tensor_copy(o, i), waits=ew, inc=evs)
                in_ev[(b, slot)] = tin[j]
            qv = _fp4_quant_tile(P, tmw[:, b * W:(b + 1) * W], aq, lvl, lvl2, scr, W,
                                 evs, sqs, [(evs, tin[3])])
            prev_q[b] = qv
            for j in range(4):
                slot = j % 2
                bk = P.bank(2 + slot, n=128)
                wv = [(evs, qv)]
                if j >= 2:
                    wv.append((evs, out_ev[(b, slot)]))
                elif (b, slot) in out_ev:
                    wv.append((evs, out_ev[(b, slot)]))
                m = P.trans(bk, tmw[:, b * W + j * 128: b * W + (j + 1) * 128],
                            ident[:], waits=wv, inc=mms)
                prev_otr[b] = m
                ew = [(mms, m)]
                if (b, slot) in out_st:
                    ew.append(out_st[(b, slot)])
                ev = P.vop(lambda e, o=stg[:, (b * 2 + slot) * 128:(b * 2 + slot + 1) * 128],
                           i=bk: e.tensor_copy(o, i), waits=ew, inc=evs)
                out_ev[(b, slot)] = ev
                out_st[(b, slot)] = P.store(("sync", "scalar")[j % 2],
                                            dst_dram[db * W + j * 128: db * W + (j + 1) * 128,
                                                     tc * 128:(tc + 1) * 128],
                                            stg[:, (b * 2 + slot) * 128:(b * 2 + slot + 1) * 128],
                                            waits=[(evs, ev)], sem=f"so{b * 4 + j}")
    P.end_stage()


def emit_l1_attn(P, k_dram, q_dram, vt_dram, o_dram):
    P.begin_stage("l1at")
    kh = P.scratch("kh", [128, T], F32R)
    qh = P.scratch("qh", [128, T], F32R)
    vtb = P.scratch("vtb", [128, 32 * 128], F32R)
    attn = P.scratch("attn", [128, 16 * TT], F32R)
    shift2 = P.scratch("shift2", [2, T], F32R)
    ksum = P.scratch("ksum", [128, 2], F32R)
    ksf = P.scratch("ksf", [128, 1], F32)
    dden = P.scratch("dden", [2, TT], F32R)
    ddf = P.scratch("ddf", [1, TT], F32)
    avs = P.scratch("avs", [128, 2 * TT], F32R)
    stg = P.scratch("stg", [128, 2 * TT], F32R)
    mms = P.sem("mm_s")
    evs = P.sem("ev_s")
    sqs = P.sem("sq_s")
    hv_prev = None
    st_prev = {}
    exp_prev = {}
    av_copy_prev = None
    r1_prev = None
    mul_prev = None
    for h in range(L1_H):
        pw = [] if hv_prev is None else [(mms, hv_prev[0]), (evs, hv_prev[1])]
        lk = P.load("sync", kh[:], k_dram[h * 128:(h + 1) * 128, :], waits=pw, sem="x0")
        lq = P.load("scalar", qh[:], q_dram[h * 128:(h + 1) * 128, :], waits=pw, sem="x1")
        lv = []
        for sc in range(32):
            lv.append(P.load("gpsimd", vtb[:, sc * 128:(sc + 1) * 128],
                             vt_dram[sc * 128:(sc + 1) * 128, h * 128:(h + 1) * 128],
                             waits=pw if sc == 0 else (), sem=f"v{sc % 8}"))
        kz = P.vop(lambda e, o=ksum[:, 0:2].bitcast(F32): e.memset(o, 0.0), waits=pw)
        P.vop(lambda e, o=ksf[:, 0:1], i=kh[:].bitcast(F32):
              e.reduce_sum(o, i, mybir.AxisListType.X), waits=[lk])
        kv = P.vop(lambda e, o=ksum[:, 0:1], i=ksf[:, 0:1]:
                   e.tensor_scalar(o, i, -1.0 / (3.0 * T), None, ALU.mult), inc=evs)
        z2 = P.zero2(shift2[0:2, :])
        sh_ev = {}
        for t in range(NTT):
            wv = [(evs, kv), lq]
            if mul_prev is not None:
                wv.append((evs, mul_prev))
            if t - 1 in sh_ev:
                wv.append((evs, sh_ev[t - 1]))
            m = P.mm(P.bank(6)[0:2, :], ksum[:, 0:2], qh[:, t * TT:(t + 1) * TT],
                     start=True, stop=True, waits=wv, inc=mms)
            sh_ev[t] = P.vop(lambda e, o=shift2[0:1, t * TT:(t + 1) * TT],
                             i=P.bank(6)[0:2, :][0:1, :]: e.tensor_copy(o, i),
                             waits=[(mms, m)], inc=evs)
        shdone = evs.n
        for t in range(NTT):
            slot_den_mm = {}
            for half in range(2):
                for s16 in range(16):
                    sc = half * 16 + s16
                    bk = P.bank(sc % 2)
                    at = attn[:, s16 * TT:(s16 + 1) * TT]
                    pm_w = [(evs, shdone), lk, lq]
                    if (sc % 2) in exp_prev:
                        pm_w.append((sqs, exp_prev[sc % 2]))
                    P.mm(bk, P.ones[0:2, :], shift2[0:2, t * TT:(t + 1) * TT],
                         start=True, stop=False, waits=pm_w)
                    m = P.mm(bk, kh[:, sc * 128:(sc + 1) * 128], qh[:, t * TT:(t + 1) * TT],
                             start=False, stop=True, inc=mms)
                    ev_w = [(mms, m)]
                    if half == 1:
                        ev_w.append((mms, slot_den_mm[s16]))
                    ex = P.act(at, bk, AF.Exp, scale=1.5, waits=ev_w, inc=sqs)
                    exp_prev[sc % 2] = ex
                    avw = [(sqs, ex), lv[sc]]
                    if sc == 0 and av_copy_prev is not None:
                        avw.append((evs, av_copy_prev))
                    if sc == 0 and r1_prev is not None:
                        avw.append((evs, r1_prev))
                    P.mm(P.bank(4), vtb[:, sc * 128:(sc + 1) * 128], at,
                         start=(sc == 0), stop=(sc == 31), waits=avw)
                    slot_den_mm[s16] = P.mm(P.bank(5)[0:2, :], P.ones[:, 0:2], at,
                                            start=(sc == 0), stop=(sc == 31), inc=mms)
            dn = mms.n
            P.zero2(dden[0:2, :])
            P.vop(lambda e, o=ddf[0:1, :], i=P.bank(5)[0:2, :][0:1, :]:
                  e.reciprocal(o, i), waits=[(mms, dn)])
            r1 = P.vop(lambda e, o=dden[0:1, :], i=ddf[0:1, :]: e.tensor_copy(o, i), inc=evs)
            r1_prev = r1
            bw = [(evs, r1)]
            if mul_prev is not None:
                bw.append((evs, mul_prev))
            bm = P.mm(P.bank(6), P.ones[0:2, :], dden[0:2, :], start=True, stop=True,
                      waits=bw, inc=mms)
            b_av = t % 2
            cw = [(mms, dn)]
            if b_av in st_prev:
                cw.append(st_prev[b_av])
            c1 = P.vop(lambda e, o=avs[:, b_av * TT:(b_av + 1) * TT], i=P.bank(4):
                       e.tensor_copy(o, i), waits=cw, inc=evs)
            av_copy_prev = c1
            ev = P.vop(lambda e, o=stg[:, b_av * TT:(b_av + 1) * TT],
                       x=avs[:, b_av * TT:(b_av + 1) * TT].bitcast(F32), y=P.bank(6):
                       e.tensor_tensor(o, x, y, ALU.mult), waits=[(mms, bm)], inc=evs)
            mul_prev = ev
            st_prev[b_av] = P.store("sync", o_dram[h * 128:(h + 1) * 128, t * TT:(t + 1) * TT],
                                    stg[:, b_av * TT:(b_av + 1) * TT], waits=[(evs, ev)],
                                    sem=f"so{b_av}")
        hv_prev = (mms.n, evs.n)
    P.end_stage()


def emit_layer1(P, x_in, x_out):
    xn = P.dram("l1_xn", [DM, T], F32R)
    emit_layernorm(P, "l1ln", x_in, xn)
    ctx = P.dram("L1_attn_context", [768, T], F32R, kind="ExternalInput")
    k1 = P.dram("l1_k", [DM, T], F32R)
    vt1 = P.dram("l1_vt", [T, DM], F32R)
    q1 = P.dram("l1_q", [DM, T], F32R)
    o1 = P.dram("l1_o", [DM, T], F32R)
    s1 = P.dram("l1_s", [DM, T], F32R)
    emit_proj(P, "l1k", P.dram("L1_attn_Wk", [768, DM], F32R, kind="ExternalInput"),
              ctx, k1, 768, DM)
    emit_projT(P, "l1v", P.dram("L1_attn_Wv", [768, DM], F32R, kind="ExternalInput"),
               ctx, vt1, 768, DM)
    emit_proj(P, "l1q", P.dram("L1_attn_Wq", [DM, DM], F32R, kind="ExternalInput"),
              xn, q1, DM, DM, epi={"func": AF.Copy, "scale": float(128 ** -0.5)})
    emit_l1_attn(P, k1, q1, vt1, o1)
    emit_proj(P, "l1wo", P.dram("L1_attn_Wout", [DM, DM], F32R, kind="ExternalInput"),
              o1, s1, DM, DM,
              epi={"bias": P.dram("L1_attn_bout", [DM], F32R, kind="ExternalInput"),
                   "func": AF.Copy})
    OP = 3072
    wqg = P.dram("l1_wqg", [DM, OP], F32R)
    wqu = P.dram("l1_wqu", [DM, OP], F32R)
    wqd = P.dram("l1_wqd", [OP, DM], F32R)
    emit_fp4_quant_w(P, "l1qwg", P.dram("L1_ffn_w_gate", [2816, DM], F32R, kind="ExternalInput"),
                     wqg, 2816, DM, Opad=OP)
    emit_fp4_quant_w(P, "l1qwu", P.dram("L1_ffn_w_up", [2816, DM], F32R, kind="ExternalInput"),
                     wqu, 2816, DM, Opad=OP)
    emit_fp4_quant_w(P, "l1qwd", P.dram("L1_ffn_w_down", [DM, 2816], F32R, kind="ExternalInput"),
                     wqd, DM, 2816, Dpad=OP, width=256)
    xq = P.dram("l1_xq", [DM, T], F32R)
    emit_fp4_quant_act(P, "l1xq", xn, xq, DM, T)
    gate = P.dram("l1_gate", [OP, T], F32R)
    up = P.dram("l1_up", [OP, T], F32R)
    h1 = P.dram("l1_h", [OP, T], F32R)
    hq = P.dram("l1_hq", [OP, T], F32R)
    g1 = P.dram("l1_g", [DM, T], F32R)
    emit_proj(P, "l1fg", wqg, xq, gate, DM, OP, epi={"func": AF.Silu})
    emit_proj(P, "l1fu", wqu, xq, up, DM, OP)

    def hbody(P, tiles, out, wv, evs, dc, t):
        a, b = tiles
        return P.vop(lambda e, o=out, x=a, y=b: e.tensor_tensor(o, x, y, ALU.mult),
                     waits=wv, inc=evs)
    emit_ew(P, "l1h", [gate, up], h1, hbody, OP)
    emit_fp4_quant_act(P, "l1hq", h1, hq, OP, T)
    emit_proj(P, "l1fd", wqd, hq, g1, OP, DM, epi={"func": AF.Silu})
    emit_xupdate(P, "l1xu", x_in, s1, g1, x_out)



def _phi_gate(P, tmpname, scratch_cache):
    def gate_fn(Pp, bank_ap, sa, orow, t, wv, evs):
        if tmpname not in scratch_cache:
            scratch_cache[tmpname] = Pp.scratch(tmpname, [128, 2 * TT], F32)
        tmp = scratch_cache[tmpname]
        sq = Pp.sem("sq_s")
        v1 = Pp.vop(lambda e, o=tmp[:, 0:TT], i=bank_ap:
                    e.tensor_single_scalar(o, i, 0.0, ALU.min), waits=wv, inc=Pp.sem("ev_s"))
        s1 = Pp.act(tmp[:, TT:2 * TT], tmp[:, 0:TT], AF.Exp,
                    waits=[(Pp.sem("ev_s"), v1)], inc=sq)
        Pp.vop(lambda e, o=tmp[:, 0:TT], i=bank_ap:
               e.tensor_single_scalar(o, i, 0.0, ALU.max), waits=[(sq, s1)])
        return Pp.vop(lambda e, o=sa, x=tmp[:, 0:TT], y=tmp[:, TT:2 * TT]:
                      e.tensor_tensor(o, x, y, ALU.add), inc=evs)
    return gate_fn


L2_H = 16


def emit_l2_attn(P, q_dram, k_dram, kt_dram, vt_dram, o_dram):
    P.begin_stage("l2at")
    qp = P.scratch("qp", [128, TT], F32R)
    kp = P.scratch("kp", [128, TT], F32R)
    ktp = P.scratch("ktp", [128, 4 * 128], F32R)
    vtp = P.scratch("vtp", [128, 4 * 128], F32R)
    attn = P.scratch("attn", [128, 4 * TT], F32R)
    S = P.scratch("S", [128, 8 * 64], F32R)
    zbuf = P.scratch("zbuf", [128, 16], F32R)
    zf = P.scratch("zf", [128, 1], F32)
    zS = P.scratch("zS", [128, 8 * 64], F32)
    masks = P.scratch("masks", [128, 4 * TT], F32)
    iot = P.scratch("iot", [128, TT], F32)
    icol = P.scratch("icol", [128, 1], F32)
    dden = P.scratch("dden", [2, TT], F32R)
    ddf = P.scratch("ddf", [1, TT], F32)
    numb = P.scratch("numb", [64, 2 * TT], F32R)
    stg = P.scratch("stg", [64, 2 * TT], F32R)
    mms = P.sem("mm_s")
    evs = P.sem("ev_s")
    dr = P.sem("gp_done")
    P.vop(lambda e: e.iota(iot[:], [[1, TT]], base=0, channel_multiplier=0,
                           allow_small_or_imprecise_dtypes=True), eng="gpsimd")
    P.vop(lambda e: e.iota(icol[:], [[0, 1]], base=0, channel_multiplier=1,
                           allow_small_or_imprecise_dtypes=True), eng="gpsimd")
    P.rec("gpsimd", lambda e: e.drain())
    P.rec("gpsimd", lambda e: e.nop(), inc=dr)
    P.pending["vector"].append((dr, dr.n))
    for so in range(4):
        ma = masks[:, so * TT:(so + 1) * TT]
        P.vop(lambda e, o=ma, i=iot[:], sh=-so * 128.0:
              e.tensor_scalar(o, i, 1.0, sh, ALU.mult, ALU.add))
        P.vop(lambda e, o=ma: e.tensor_scalar(o, o, icol[:, 0:1], None, ALU.subtract))
        P.vop(lambda e, o=ma: e.tensor_single_scalar(o, o, 0.0, ALU.is_ge))
    P.vop(lambda e: e.memset(zS[:], 0.0))
    P.vop(lambda e: e.memset(zf[:], 0.0))
    P.vop(lambda e: e.tensor_copy(S[:], zS[:]))
    P.vop(lambda e: e.tensor_copy(zbuf[:], zS[:, 0:16]))
    zs = P.vop(lambda e: e.engine_nop(), inc=evs)
    P.pending["tensor"].append((evs, zs))
    st_prev = {}
    for c in range(NTT):
        for hp in range(8):
            pw = [(mms, mms.n), (evs, evs.n)] if (c or hp) else []
            lq = P.load("sync", qp[:], q_dram[hp * 128:(hp + 1) * 128, c * TT:(c + 1) * TT],
                        waits=pw, sem="x0")
            lk = P.load("scalar", kp[:], k_dram[hp * 128:(hp + 1) * 128, c * TT:(c + 1) * TT],
                        waits=pw, sem="x1")
            lkt, lvt = [], []
            for tc in range(4):
                lkt.append(P.load("gpsimd", ktp[:, tc * 128:(tc + 1) * 128],
                                  kt_dram[c * TT + tc * 128:c * TT + (tc + 1) * 128,
                                          hp * 128:(hp + 1) * 128],
                                  waits=pw if tc == 0 else (), sem=f"x{2 + tc}"))
                lvt.append(P.load("gpsimd", vtp[:, tc * 128:(tc + 1) * 128],
                                  vt_dram[c * TT + tc * 128:c * TT + (tc + 1) * 128,
                                          hp * 128:(hp + 1) * 128],
                                  waits=pw if tc == 0 else (), sem=f"x{6 + tc}"))
            for hh in range(2):
                h = hp * 2 + hh
                hb = hh * 64
                p2 = 2 * (h // 2)
                sv = {}
                for sc in range(4):
                    sv[sc] = P.mm(P.bank(sc),
                                  kp[hb:hb + 64, sc * 128:(sc + 1) * 128],
                                  qp[hb:hb + 64, :],
                                  start=True, stop=True,
                                  waits=[lq, lk] if sc == 0 else (), inc=mms)
                av = {}
                for sc in range(4):
                    av[sc] = P.vop(lambda e, o=attn[:, sc * TT:(sc + 1) * TT], a=P.bank(sc),
                                   m=masks[:, sc * TT:(sc + 1) * TT]:
                                   e.tensor_tensor(o, a, m, ALU.mult),
                                   waits=[(mms, sv[sc])], inc=evs)
                for sc in range(4):
                    P.mm(P.bank(4)[0:64, :],
                         vtp[:, sc * 128:(sc + 1) * 128][:, hb:hb + 64],
                         attn[:, sc * TT:(sc + 1) * TT],
                         start=(sc == 0), stop=False,
                         waits=[lvt[sc], (evs, av[sc])])
                nv = P.mm(P.bank(4)[0:64, :],
                          S[hb:hb + 64, (h // 2) * 64:(h // 2 + 1) * 64],
                          qp[hb:hb + 64, :], start=False, stop=True, inc=mms)
                for sc in range(4):
                    P.mm(P.bank(5)[0:2, :], P.ones[:, 0:2],
                         attn[:, sc * TT:(sc + 1) * TT],
                         start=(sc == 0), stop=False,
                         waits=[(evs, av[sc])] if sc == 0 else ())
                dv = P.mm(P.bank(5)[0:2, :], zbuf[hb:hb + 64, p2:p2 + 2],
                          qp[hb:hb + 64, :], start=False, stop=True, inc=mms)
                P.zero2(dden[0:2, :])
                P.vop(lambda e, o=ddf[0:1, :], i=P.bank(5)[0:2, :][0:1, :]:
                      e.tensor_scalar(o, i, 1.0, 1e-6, ALU.mult, ALU.add),
                      waits=[(mms, dv)])
                P.vop(lambda e, o=ddf[0:1, :]: e.reciprocal(o, o))
                r1 = P.vop(lambda e, o=dden[0:1, :], i=ddf[0:1, :]: e.tensor_copy(o, i),
                           inc=evs)
                bm = P.mm(P.bank(6)[0:64, :], P.ones[0:2, 0:64], dden[0:2, :],
                          start=True, stop=True, waits=[(evs, r1)], inc=mms)
                bo = (c * 16 + h) % 2
                c1 = P.vop(lambda e, o=numb[:, bo * TT:(bo + 1) * TT], i=P.bank(4)[0:64, :]:
                           e.tensor_copy(o, i), waits=[(mms, nv)], inc=evs)
                ev = P.vop(lambda e, o=stg[:, bo * TT:(bo + 1) * TT],
                           x=numb[:, bo * TT:(bo + 1) * TT].bitcast(F32),
                           y=P.bank(6)[0:64, :]:
                           e.tensor_tensor(o, x, y, ALU.mult), waits=[(mms, bm)], inc=evs)
                swv = [(evs, ev)]
                if bo in st_prev:
                    swv.append(st_prev[bo])
                st_prev[bo] = P.store("sync",
                                      o_dram[h * 64:(h + 1) * 64, c * TT:(c + 1) * TT],
                                      stg[:, bo * TT:(bo + 1) * TT],
                                      waits=swv, sem=f"so{bo}")
                uv = None
                for tc in range(4):
                    uv = P.mm(P.bank(7)[0:64, 0:64],
                              ktp[:, tc * 128:(tc + 1) * 128][:, hb:hb + 64],
                              vtp[:, tc * 128:(tc + 1) * 128][:, hb:hb + 64],
                              start=(tc == 0), stop=(tc == 3),
                              waits=[lkt[tc]] if tc == 0 else (),
                              inc=mms if tc == 3 else None)
                P.vop(lambda e, o=S[hb:hb + 64, (h // 2) * 64:(h // 2 + 1) * 64],
                      i=P.bank(7)[0:64, 0:64]:
                      e.scalar_tensor_tensor(o, o.bitcast(F32), 1.0, i, ALU.mult, ALU.add),
                      waits=[(mms, uv), (mms, nv)], inc=evs)
                P.vop(lambda e, i=kp[hb:hb + 64, :].bitcast(F32), o=zf[hb:hb + 64, 0:1]:
                      e.reduce_sum(o, i, mybir.AxisListType.X))
                P.vop(lambda e, o=zbuf[hb:hb + 64, p2:p2 + 1], z=zf[hb:hb + 64, 0:1]:
                      e.scalar_tensor_tensor(o, o.bitcast(F32), 1.0, z,
                                             ALU.mult, ALU.add),
                      waits=[(mms, dv)], inc=evs)
                P.pending["tensor"].append((evs, evs.n))
    P.end_stage()


def emit_l2_router(P, xn_dram, wr_dram, gates_fm_dram):
    P.begin_stage("l2rt")
    ident = make_identity(P)
    wrb = P.scratch("wrb", [128, 8 * 8], F32R)
    xb = P.scratch("xb", [128, 2 * TT], F32R)
    lgf = P.scratch("lgf", [8, T], F32R)
    lg = P.scratch("lg", [128, 8], F32)
    pr = P.scratch("pr", [128, 8], F32)
    cols = P.scratch("cols", [128, 6], F32)
    gtm = P.scratch("gtm", [128, 8], F32)
    gfm = P.scratch("gfm", [8, 128], F32R)
    mms = P.sem("mm_s")
    evs = P.sem("ev_s")
    sqs = P.sem("sq_s")
    lw = []
    for dc in range(8):
        lw.append(P.load("gpsimd", wrb[:, dc * 8:(dc + 1) * 8],
                         wr_dram[dc * 128:(dc + 1) * 128, :], sem="w0"))
    xfree = [None, None]
    ev0 = None
    for t in range(NTT):
        mv = None
        for dc in range(8):
            b = (t * 8 + dc) % 2
            lx = P.load(("sync", "scalar")[dc % 2], xb[:, b * TT:(b + 1) * TT],
                        xn_dram[dc * 128:(dc + 1) * 128, t * TT:(t + 1) * TT],
                        waits=[(mms, xfree[b])] if xfree[b] is not None else (),
                        sem=f"x{b}")
            mv = P.mm(P.bank(0)[0:8, :], wrb[:, dc * 8:(dc + 1) * 8],
                      xb[:, b * TT:(b + 1) * TT],
                      start=(dc == 0), stop=(dc == 7),
                      waits=[lx, lw[dc]] + ([(evs, ev0)] if (dc == 0 and ev0 is not None) else []),
                      inc=mms)
            xfree[b] = mv
        ev0 = P.vop(lambda e, o=lgf[0:8, t * TT:(t + 1) * TT], i=P.bank(0)[0:8, :]:
                    e.tensor_copy(o, i), waits=[(mms, mv)], inc=evs)
    lgdone = evs.n
    prev = None
    prev_st = None
    for tc in range(32):
        m1 = P.trans(P.bank(1)[0:128, 0:8], lgf[0:8, tc * 128:(tc + 1) * 128].bitcast(F32),
                     ident[0:8, 0:8],
                     waits=[(evs, lgdone)] + ([(evs, prev)] if prev is not None else []),
                     inc=mms)
        P.vop(lambda e, o=lg[:], i=P.bank(1)[0:128, 0:8]: e.tensor_copy(o, i),
              waits=[(mms, m1)])
        P.vop(lambda e, o=cols[:, 5:6], i=lg[:]:
              e.reduce_max(o, i, mybir.AxisListType.X))
        P.vop(lambda e, o=cols[:, 0:1], i=cols[:, 5:6]:
              e.tensor_scalar(o, i, -1.0, None, ALU.mult))
        vv = P.vop(lambda e: e.engine_nop(), inc=evs)
        s1 = P.act(pr[:], lg[:], AF.Exp, bias=cols[:, 0:1], waits=[(evs, vv)], inc=sqs)
        P.vop(lambda e, o=cols[:, 1:2], i=pr[:]:
              e.reduce_sum(o, i, mybir.AxisListType.X), waits=[(sqs, s1)])
        P.vop(lambda e, o=cols[:, 2:3], i=cols[:, 1:2]: e.reciprocal(o, i))
        P.vop(lambda e, o=pr[:], i=pr[:]:
              e.tensor_scalar(o, i, cols[:, 2:3], None, ALU.mult))
        P.vop(lambda e, o=cols[:, 3:4], i=pr[:]:
              e.reduce_max(o, i, mybir.AxisListType.X))
        P.vop(lambda e, o=lg[:], i=pr[:]:
              e.tensor_scalar(o, i, cols[:, 3:4], None, ALU.is_lt))
        P.vop(lambda e, o=lg[:], x=lg[:], y=pr[:]: e.tensor_tensor(o, x, y, ALU.mult))
        P.vop(lambda e, o=cols[:, 4:5], i=lg[:]:
              e.reduce_max(o, i, mybir.AxisListType.X))
        P.vop(lambda e, o=lg[:], i=pr[:]:
              e.tensor_scalar(o, i, cols[:, 4:5], None, ALU.is_ge))
        gv = P.vop(lambda e, o=gtm[:], x=pr[:], y=lg[:]:
                   e.tensor_tensor(o, x, y, ALU.mult), inc=evs)
        m = P.trans(P.bank(2)[0:8, 0:128], gtm[:], ident[:], waits=[(evs, gv)], inc=mms)
        cw = [(mms, m)]
        if prev_st is not None:
            cw.append(prev_st)
        prev = P.vop(lambda e, o=gfm[:], i=P.bank(2)[0:8, 0:128]: e.tensor_copy(o, i),
                     waits=cw, inc=evs)
        prev_st = P.store("sync", gates_fm_dram[0:8, tc * 128:(tc + 1) * 128], gfm[:],
                          waits=[(evs, prev)], sem="so0")
    P.end_stage()


def emit_moe_bias_init(P, gates_fm_dram, b2_dram, yacc_dram):
    P.begin_stage("l2bi")
    b2b = P.scratch("b2b", [8, DM], F32R)
    gb = P.scratch("gb", [8, T], F32R)
    stg = P.scratch("stg", [128, 2 * TT], F32R)
    mms = P.sem("mm_s")
    evs = P.sem("ev_s")
    lb = P.load("sync", b2b[:], b2_dram, sem="w0")
    lgt = P.load("scalar", gb[:], gates_fm_dram, sem="w1")
    prev_st = {}
    for t in range(NTT):
        for dc in range(8):
            bk = (t * 8 + dc) % 2
            m = P.mm(P.bank(bk), b2b[:, dc * 128:(dc + 1) * 128],
                     gb[:, t * TT:(t + 1) * TT], start=True, stop=True,
                     waits=[lb, lgt] + ([(evs, prev_st[bk][0])] if bk in prev_st else []),
                     inc=mms)
            ev = P.vop(lambda e, o=stg[:, bk * TT:(bk + 1) * TT], i=P.bank(bk):
                       e.tensor_copy(o, i),
                       waits=[(mms, m)] + ([prev_st[bk][1]] if bk in prev_st else []), inc=evs)
            st = P.store("sync", yacc_dram[dc * 128:(dc + 1) * 128, t * TT:(t + 1) * TT],
                         stg[:, bk * TT:(bk + 1) * TT], waits=[(evs, ev)], sem=f"so{bk}")
            prev_st[bk] = (ev, st)
    P.end_stage()


def emit_layer2(P, x_in, x_out):
    xn = P.dram("l2_xn", [DM, T], F32R)
    emit_layernorm(P, "l2ln", x_in, xn)
    q2 = P.dram("l2_q", [DM, T], F32R)
    k2 = P.dram("l2_k", [DM, T], F32R)
    kt2 = P.dram("l2_kt", [T, DM], F32R)
    vt2 = P.dram("l2_vt", [T, DM], F32R)
    o2 = P.dram("l2_o", [DM, T], F32R)
    s2 = P.dram("l2_s", [DM, T], F32R)
    cache = {}
    emit_proj(P, "l2q", P.dram("L2_attn_Wq", [DM, DM], F32R, kind="ExternalInput"),
              xn, q2, DM, DM, epi={"gate_fn": _phi_gate(P, "phtmp", cache)})
    cache.clear()
    emit_proj(P, "l2k", P.dram("L2_attn_Wk", [DM, DM], F32R, kind="ExternalInput"),
              xn, k2, DM, DM, epi={"gate_fn": _phi_gate(P, "phtmp", cache)})
    cache.clear()

    def phi_post(Pp, bank_ap, sa, trow, wvv, evs2):
        return _phi_gate(Pp, "phtmp", cache)(Pp, bank_ap, sa, trow, 0, wvv, evs2)
    emit_projT(P, "l2kt", P.dram("L2_attn_Wk", [DM, DM], F32R, kind="ExternalInput"),
               xn, kt2, DM, DM, epi={"post": phi_post})
    cache.clear()
    emit_projT(P, "l2vt", P.dram("L2_attn_Wv", [DM, DM], F32R, kind="ExternalInput"),
               xn, vt2, DM, DM)
    emit_l2_attn(P, q2, k2, kt2, vt2, o2)
    emit_proj(P, "l2wo", P.dram("L2_attn_Wo", [DM, DM], F32R, kind="ExternalInput"),
              o2, s2, DM, DM)
    gates = P.dram("l2_gates", [8, T], F32R)
    emit_l2_router(P, xn, P.dram("L2_ffn_w_router", [DM, 8], F32R, kind="ExternalInput"),
                   gates)
    yacc = P.dram("l2_yacc", [DM, T], F32R)
    emit_moe_bias_init(P, gates, P.dram("L2_ffn_b2", [8, DM], F32R, kind="ExternalInput"),
                       yacc)
    he = P.dram("l2_he", [4096, T], F32R)
    W1 = P.dram("L2_ffn_W1", [8 * DM, 4096], F32R, kind="ExternalInput")
    W2 = P.dram("L2_ffn_W2", [8 * 4096, DM], F32R, kind="ExternalInput")
    b1 = P.dram("L2_ffn_b1", [8, 4096], F32R, kind="ExternalInput")
    emit_proj(P, "l2sh1", P.dram("L2_ffn_Ws1", [DM, 4096], F32R, kind="ExternalInput"),
              xn, he, DM, 4096,
              epi={"func": AF.Gelu_apprx_tanh,
                   "bias": P.dram("L2_ffn_bs1", [4096], F32R, kind="ExternalInput")})
    emit_proj(P, "l2sh2", P.dram("L2_ffn_Ws2", [4096, DM], F32R, kind="ExternalInput"),
              he, yacc, 4096, DM, accum_out=True)
    for e_i in range(8):
        emit_proj(P, f"l2e{e_i}a", W1[e_i * DM:(e_i + 1) * DM, :], xn, he, DM, 4096,
                  epi={"func": AF.Gelu_apprx_tanh, "bias": b1[e_i]})
        emit_proj(P, f"l2e{e_i}b", W2[e_i * 4096:(e_i + 1) * 4096, :], he, yacc, 4096, DM,
                  accum_out=True, epi={"colmul": gates[e_i]})
    g2 = P.dram("l2_g", [DM, T], F32R)
    SC2 = float(1.0 / np.sqrt(1.0 + 2.0 / 8.0))

    def gbody(Pp, tiles, out, wvv, evs2, dc, t):
        return Pp.act(out, tiles[0], AF.Silu, scale=SC2,
                      bias=gbody.bias[:, dc:dc + 1], waits=wvv, inc=evs2)
    emit_ew_bias = None
    P.begin_stage("l2g")
    bs2cols, _ = load_bias_cols(P, P.dram("L2_ffn_bs2", [DM], F32R, kind="ExternalInput"), 8)
    bs2s = P.scratch("bs2s", [128, 8], F32)
    P.vop(lambda e: e.tensor_scalar(bs2s[:], bs2cols[:, 0:8], SC2, None, ALU.mult))
    tiles = P.scratch("in", [128, 2 * TT], F32)
    ot = P.scratch("out", [128, 2 * TT], F32R)
    evs = P.sem("ev_s")
    prev_ev = [None, None]
    prev_st2 = [None, None]
    idx = 0
    for t in range(NTT):
        for dc in range(8):
            b = idx % 2
            wv = [] if prev_ev[b] is None else [(evs, prev_ev[b])]
            lv = P.load(("sync", "scalar")[idx % 2], tiles[:, b * TT:(b + 1) * TT],
                        yacc[dc * 128:(dc + 1) * 128, t * TT:(t + 1) * TT].bitcast(F32),
                        waits=wv, sem=f"x{b}")
            wv2 = [lv]
            if prev_st2[b] is not None:
                wv2.append(prev_st2[b])
            ev = P.act(ot[:, b * TT:(b + 1) * TT], tiles[:, b * TT:(b + 1) * TT],
                       AF.Silu, scale=SC2, bias=bs2s[:, dc:dc + 1], waits=wv2, inc=evs)
            prev_ev[b] = ev
            prev_st2[b] = P.store("sync", g2[dc * 128:(dc + 1) * 128, t * TT:(t + 1) * TT],
                                  ot[:, b * TT:(b + 1) * TT], waits=[(evs, ev)], sem=f"so{b}")
            idx += 1
    P.end_stage()
    emit_xupdate(P, "l2xu", x_in, s2, g2, x_out)



def emit_small_proj(P, name, w_dram, x_dram, out_dram, D, O):
    nd = D // 128
    P.begin_stage(name)
    wb = P.scratch("wb", [128, nd * O], F32R)
    xb = P.scratch("xb", [128, 2 * TT], F32R)
    stg = P.scratch("stg", [128, 2 * TT], F32R)
    mms = P.sem("mm_s")
    evs = P.sem("ev_s")
    lw = []
    for dc in range(nd):
        lw.append(P.load("gpsimd", wb[:, dc * O:(dc + 1) * O],
                         w_dram[dc * 128:(dc + 1) * 128, :], sem="w0"))
    xfree = [None, None]
    prev_ev = {}
    prev_st = {}
    for t in range(NTT):
        bp = t % 2
        mv = None
        for dc in range(nd):
            b = dc % 2
            lx = P.load(("sync", "scalar")[b], xb[:, b * TT:(b + 1) * TT],
                        x_dram[dc * 128:(dc + 1) * 128, t * TT:(t + 1) * TT],
                        waits=[(mms, xfree[b])] if xfree[b] is not None else (),
                        sem=f"x{b}")
            mv = P.mm(P.bank(bp)[0:O, :], wb[:, dc * O:(dc + 1) * O],
                      xb[:, b * TT:(b + 1) * TT],
                      start=(dc == 0), stop=(dc == nd - 1),
                      waits=[lx, lw[dc]] + ([(evs, prev_ev[bp])]
                                            if (dc == 0 and bp in prev_ev) else []),
                      inc=mms)
            xfree[b] = mv
        ev = P.vop(lambda e, o=stg[0:O, bp * TT:(bp + 1) * TT], i=P.bank(bp)[0:O, :]:
                   e.tensor_copy(o, i),
                   waits=[(mms, mv)] + ([prev_st[bp]] if bp in prev_st else []), inc=evs)
        prev_ev[bp] = ev
        prev_st[bp] = P.store("sync", out_dram[0:O, t * TT:(t + 1) * TT],
                              stg[0:O, bp * TT:(bp + 1) * TT],
                              waits=[(evs, ev)], sem=f"so{bp}")
    P.end_stage()


def emit_l3_conv(P, xz_dram, convw_dram, convb_dram, u_dram):
    P.begin_stage("l3cv")
    wc = P.scratch("wc", [128, 16 * 4], F32)
    lw = []
    for dc in range(16):
        lw.append(P.load("gpsimd", wc[:, dc * 4:(dc + 1) * 4],
                         convw_dram[dc * 128:(dc + 1) * 128, :].bitcast(F32), sem="w0"))
    bcols, _ = load_bias_cols(P, convb_dram, 16, name="cb")
    W3 = TT + 3
    xt = P.scratch("xt", [128, 2 * W3], F32)
    acc = P.scratch("acc", [128, 2 * TT], F32)
    stg = P.scratch("stg", [128, 2 * TT], F32R)
    evs = P.sem("ev_s")
    sqs = P.sem("sq_s")
    prev_ev = {}
    prev_sq = {}
    prev_st = {}
    idx = 0
    for dc in range(16):
        for t in range(NTT):
            b = idx % 2
            wv = [] if b not in prev_ev else [(evs, prev_ev[b])]
            if t == 0:
                zv = P.vop(lambda e, o=xt[:, b * W3:b * W3 + 3]: e.memset(o, 0.0),
                           waits=wv, inc=evs)
                lx = P.load("sync", xt[:, b * W3 + 3:b * W3 + 3 + TT],
                            xz_dram[dc * 128:(dc + 1) * 128, 0:TT].bitcast(F32),
                            waits=[(evs, zv)], sem=f"x{b}")
            else:
                lx = P.load("sync", xt[:, b * W3:b * W3 + W3],
                            xz_dram[dc * 128:(dc + 1) * 128,
                                    t * TT - 3:t * TT + TT].bitcast(F32),
                            waits=wv, sem=f"x{b}")
            wv0 = [lx, lw[dc]]
            if b in prev_sq:
                wv0.append((sqs, prev_sq[b]))
            P.vop(lambda e, o=acc[:, b * TT:(b + 1) * TT],
                  i=xt[:, b * W3:b * W3 + TT], w=wc[:, dc * 4:dc * 4 + 1]:
                  e.tensor_scalar(o, i, w, None, ALU.mult), waits=wv0)
            vv = None
            for j in range(1, 4):
                vv = P.vop(lambda e, o=acc[:, b * TT:(b + 1) * TT],
                           i=xt[:, b * W3 + j:b * W3 + j + TT],
                           w=wc[:, dc * 4 + j:dc * 4 + j + 1],
                           a=acc[:, b * TT:(b + 1) * TT]:
                           e.scalar_tensor_tensor(o, i, w, a, ALU.mult, ALU.add),
                           inc=evs if j == 3 else None)
            ev = P.act(stg[:, b * TT:(b + 1) * TT], acc[:, b * TT:(b + 1) * TT],
                       AF.Silu, bias=bcols[:, dc:dc + 1],
                       waits=[(evs, vv)] + ([prev_st[b]] if b in prev_st else []), inc=sqs)
            prev_ev[b] = vv
            prev_sq[b] = ev
            prev_st[b] = P.store("sync", u_dram[dc * 128:(dc + 1) * 128, t * TT:(t + 1) * TT],
                                 stg[:, b * TT:(b + 1) * TT], waits=[(sqs, ev)], sem=f"so{b}")
            idx += 1
    P.end_stage()


def emit_l3_bcast(P, dbc_dram, bbc_dram, cbc_dram):
    P.begin_stage("l3bc")
    E = P.scratch("E", [16, 16 * 128], F32)
    Er = P.scratch("Er", [16, 16 * 128], F32R)
    binb = P.scratch("binb", [16, 2 * T], F32R)
    stg = P.scratch("stg", [128, 2 * TT], F32R)
    mms = P.sem("mm_s")
    evs = P.sem("ev_s")
    dr = P.sem("gp_done")
    P.vop(lambda e: e.iota(E[:], [[1, 16], [0, 128]], base=0, channel_multiplier=-1,
                           allow_small_or_imprecise_dtypes=True), eng="gpsimd")
    P.rec("gpsimd", lambda e: e.drain())
    P.rec("gpsimd", lambda e: e.nop(), inc=dr)
    P.pending["vector"].append((dr, dr.n))
    P.vop(lambda e: e.tensor_single_scalar(E[:], E[:], 0.0, ALU.is_equal))
    ez = P.vop(lambda e: e.tensor_copy(Er[:], E[:]), inc=evs)
    lb = P.load("sync", binb[0:16, 0:T], dbc_dram[64:80, :], sem="x0")
    lc = P.load("scalar", binb[0:16, T:2 * T], dbc_dram[80:96, :], sem="x1")
    prev = {}
    for src_i, (dram, lsrc) in enumerate(((bbc_dram, lb), (cbc_dram, lc))):
        for n in range(16):
            for t in range(NTT):
                bk = (src_i * 16 * NTT + n * NTT + t) % 2
                wv = [lsrc, (evs, ez)]
                if bk in prev:
                    wv.append((evs, prev[bk][0]))
                m = P.mm(P.bank(bk), Er[:, n * 128:(n + 1) * 128],
                         binb[0:16, src_i * T + t * TT:src_i * T + (t + 1) * TT],
                         start=True, stop=True, waits=wv, inc=mms)
                ew = [(mms, m)]
                if bk in prev:
                    ew.append(prev[bk][1])
                ev = P.vop(lambda e, o=stg[:, bk * TT:(bk + 1) * TT], i=P.bank(bk):
                           e.tensor_copy(o, i), waits=ew, inc=evs)
                st = P.store("sync", dram[n * 128:(n + 1) * 128, t * TT:(t + 1) * TT],
                             stg[:, bk * TT:(bk + 1) * TT], waits=[(evs, ev)], sem=f"so{bk}")
                prev[bk] = (ev, st)
    P.end_stage()


def emit_l3_scan(P, delta_dram, du_dram, alog_dram, d_dram, u_dram, sz_dram,
                 bbc_dram, cbc_dram, yz_dram):
    P.begin_stage("l3sc")
    TQ = 1024
    acol = P.scratch("acol", [128, 16 * 16], F32)
    dcols, _ = load_bias_cols(P, d_dram, 16, name="dd")
    dlt = P.scratch("dlt", [128, 8 * TQ], F32)
    dub = P.scratch("dub", [128, 8 * TQ], F32)
    bcb = P.scratch("bcb", [128, TQ], F32)
    ccb = P.scratch("ccb", [128, TQ], F32)
    dA = P.scratch("dA", [128, TQ], F32)
    hb = P.scratch("hb", [128, TQ], F32)
    yac = P.scratch("yac", [128, 8 * TQ], F32)
    hlast = P.scratch("hlast", [128, 8 * 16], F32)
    ut = P.scratch("ut", [128, TQ], F32)
    szt = P.scratch("szt", [128, TQ], F32)
    stg = P.scratch("stg", [128, 2 * TQ], F32R)
    evs = P.sem("ev_s")
    sqs = P.sem("sq_s")
    la = []
    for dc in range(16):
        la.append(P.load("gpsimd", acol[:, dc * 16:(dc + 1) * 16],
                         alog_dram[dc * 128:(dc + 1) * 128, :].bitcast(F32), sem="w0"))
    sv = P.act(acol[:], acol[:], AF.Exp, waits=la, inc=sqs)
    av = P.vop(lambda e: e.tensor_scalar(acol[:], acol[:], -1.0, None, ALU.mult),
               waits=[(sqs, sv)], inc=evs)
    prev_fst = {}
    for dcg in range(2):
        P.vop(lambda e: e.memset(hlast[:], 0.0), inc=evs)
        for tq in range(4):
            ldl, ldu = [], []
            for dci in range(8):
                dc = dcg * 8 + dci
                ldl.append(P.load("sync", dlt[:, dci * TQ:(dci + 1) * TQ],
                                  delta_dram[dc * 128:(dc + 1) * 128,
                                             tq * TQ:(tq + 1) * TQ].bitcast(F32),
                                  waits=[(evs, evs.n)] if dci == 0 else (),
                                  sem=f"x{dci % 4}"))
                ldu.append(P.load("scalar", dub[:, dci * TQ:(dci + 1) * TQ],
                                  du_dram[dc * 128:(dc + 1) * 128,
                                          tq * TQ:(tq + 1) * TQ].bitcast(F32),
                                  waits=[(evs, evs.n)] if dci == 0 else (),
                                  sem=f"x{4 + dci % 4}"))
            P.vop(lambda e: e.memset(yac[:], 0.0))
            for n in range(16):
                lbc = P.load("gpsimd", bcb[:], bbc_dram[n * 128:(n + 1) * 128,
                                                        tq * TQ:(tq + 1) * TQ].bitcast(F32),
                             waits=[(evs, evs.n)], sem="v0")
                lcc = P.load("gpsimd", ccb[:], cbc_dram[n * 128:(n + 1) * 128,
                                                        tq * TQ:(tq + 1) * TQ].bitcast(F32),
                             waits=[(evs, evs.n)], sem="v1")
                for dci in range(8):
                    dc = dcg * 8 + dci
                    v1 = P.vop(lambda e, o=dA[:], i=dlt[:, dci * TQ:(dci + 1) * TQ],
                               a=acol[:, dc * 16 + n:dc * 16 + n + 1]:
                               e.tensor_scalar(o, i, a, None, ALU.mult),
                               waits=[ldl[dci], (evs, av)], inc=evs)
                    s1 = P.act(dA[:], dA[:], AF.Exp, waits=[(evs, v1)], inc=sqs)
                    P.vop(lambda e, o=hb[:], x=dub[:, dci * TQ:(dci + 1) * TQ], y=bcb[:]:
                          e.tensor_tensor(o, x, y, ALU.mult),
                          waits=[ldu[dci], lbc, (sqs, s1)])
                    P.vop(lambda e, o=hb[:], d0=dA[:], d1=hb[:],
                          ini=hlast[:, dci * 16 + n:dci * 16 + n + 1]:
                          e.tensor_tensor_scan(o, d0, d1, ini, ALU.mult, ALU.add))
                    P.vop(lambda e, o=hlast[:, dci * 16 + n:dci * 16 + n + 1],
                          i=hb[:, TQ - 1:TQ]: e.tensor_copy(o, i))
                    P.vop(lambda e, o=hb[:], x=hb[:], y=ccb[:]:
                          e.tensor_tensor(o, x, y, ALU.mult), waits=[lcc])
                    P.vop(lambda e, o=yac[:, dci * TQ:(dci + 1) * TQ], x=hb[:]:
                          e.tensor_tensor(o, x, yac[:, dci * TQ:(dci + 1) * TQ], ALU.add),
                          inc=evs)
            for dci in range(8):
                dc = dcg * 8 + dci
                b = dci % 2
                lu = P.load("sync", ut[:], u_dram[dc * 128:(dc + 1) * 128,
                                                  tq * TQ:(tq + 1) * TQ].bitcast(F32),
                            waits=[(evs, evs.n)], sem="x0")
                lz = P.load("scalar", szt[:], sz_dram[dc * 128:(dc + 1) * 128,
                                                      tq * TQ:(tq + 1) * TQ].bitcast(F32),
                            waits=[(evs, evs.n)], sem="x1")
                P.vop(lambda e, o=yac[:, dci * TQ:(dci + 1) * TQ], i=ut[:],
                      dd=dcols[:, dc:dc + 1]:
                      e.scalar_tensor_tensor(o, i, dd, yac[:, dci * TQ:(dci + 1) * TQ],
                                             ALU.mult, ALU.add), waits=[lu])
                ev = P.vop(lambda e, o=stg[:, b * TQ:(b + 1) * TQ],
                           x=yac[:, dci * TQ:(dci + 1) * TQ], y=szt[:]:
                           e.tensor_tensor(o, x, y, ALU.mult),
                           waits=[lz] + ([prev_fst[b]] if b in prev_fst else []), inc=evs)
                prev_fst[b] = P.store("sync",
                                      yz_dram[dc * 128:(dc + 1) * 128, tq * TQ:(tq + 1) * TQ],
                                      stg[:, b * TQ:(b + 1) * TQ],
                                      waits=[(evs, ev)], sem=f"so{b}")
    P.end_stage()


def emit_layer3(P, x_in, x_out):
    xn = P.dram("l3_xn", [DM, T], F32R)
    emit_layernorm(P, "l3ln", x_in, xn)
    xln = P.dram("l3_xln", [DM, T], F32R)
    emit_layernorm(P, "l3ln2", xn, xln,
                   gamma_dram=P.dram("L3_attn_ln_g", [DM], F32R, kind="ExternalInput"),
                   beta_dram=P.dram("L3_attn_ln_b", [DM], F32R, kind="ExternalInput"))
    xz = P.dram("l3_xz", [4096, T], F32R)
    emit_proj(P, "l3ip", P.dram("L3_attn_in_proj_w", [DM, 4096], F32R, kind="ExternalInput"),
              xln, xz, DM, 4096)
    u3 = P.dram("l3_u", [2048, T], F32R)
    emit_l3_conv(P, xz, P.dram("L3_attn_conv_w", [2048, 4], F32R, kind="ExternalInput"),
                 P.dram("L3_attn_conv_b", [2048], F32R, kind="ExternalInput"), u3)
    sz = P.dram("l3_sz", [2048, T], F32R)

    def szbody(Pp, tiles, out, wv, evs2, dc, t):
        return Pp.act(out, tiles[0], AF.Silu, waits=wv, inc=evs2)
    emit_ew(P, "l3sz", [xz], sz, szbody, 2048, row0s=[2048])
    dbc = P.dram("l3_dbc", [96, T], F32R)
    emit_small_proj(P, "l3xp", P.dram("L3_attn_x_proj_w", [2048, 96], F32R,
                                      kind="ExternalInput"), u3, dbc, 2048, 96)
    delta = P.dram("l3_delta", [2048, T], F32R)
    emit_proj(P, "l3dt", P.dram("L3_attn_dt_proj_w", [64, 2048], F32R, kind="ExternalInput"),
              dbc, delta, 64, 2048,
              epi={"softplus": True,
                   "bias": P.dram("L3_attn_dt_proj_b", [2048], F32R, kind="ExternalInput")})
    du = P.dram("l3_du", [2048, T], F32R)

    def dubody(Pp, tiles, out, wv, evs2, dc, t):
        return Pp.vop(lambda e, o=out, x=tiles[0], y=tiles[1]:
                      e.tensor_tensor(o, x, y, ALU.mult), waits=wv, inc=evs2)
    emit_ew(P, "l3du", [delta, u3], du, dubody, 2048)
    bbc = P.dram("l3_bbc", [16 * 128, T], F32R)
    cbc = P.dram("l3_cbc", [16 * 128, T], F32R)
    emit_l3_bcast(P, dbc, bbc, cbc)
    yz = P.dram("l3_yz", [2048, T], F32R)
    emit_l3_scan(P, delta, du, P.dram("L3_attn_A_log", [2048, 16], F32R, kind="ExternalInput"),
                 P.dram("L3_attn_D", [2048], F32R, kind="ExternalInput"),
                 u3, sz, bbc, cbc, yz)
    po = P.dram("l3_po", [DM, T], F32R)
    emit_proj(P, "l3op", P.dram("L3_attn_out_proj_w", [2048, DM], F32R, kind="ExternalInput"),
              yz, po, 2048, DM)
    s3 = P.dram("l3_s", [DM, T], F32R)

    def addbody(Pp, tiles, out, wv, evs2, dc, t):
        return Pp.vop(lambda e, o=out, x=tiles[0], y=tiles[1]:
                      e.tensor_tensor(o, x, y, ALU.add), waits=wv, inc=evs2)
    emit_ew(P, "l3s", [po, xn], s3, addbody, DM)
    a1 = P.dram("l3_a1", [4096, T], F32R)
    b1 = P.dram("l3_b1", [4096, T], F32R)
    h1 = P.dram("l3_h1", [4096, T], F32R)
    a2 = P.dram("l3_a2", [4096, T], F32R)
    a3 = P.dram("l3_a3", [4096, T], F32R)
    g3 = P.dram("l3_g", [DM, T], F32R)
    emit_proj(P, "l3w1", P.dram("L3_ffn_w1_w", [DM, 4096], F32R, kind="ExternalInput"),
              xn, a1, DM, 4096,
              epi={"func": AF.Silu, "scale_cols": P.dram("L3_ffn_w1_s", [4096], F32R,
                                                         kind="ExternalInput")})
    emit_proj(P, "l3w2", P.dram("L3_ffn_w2_w", [DM, 4096], F32R, kind="ExternalInput"),
              xn, b1, DM, 4096,
              epi={"func": AF.Copy, "scale_cols": P.dram("L3_ffn_w2_s", [4096], F32R,
                                                         kind="ExternalInput")})

    def mbody(Pp, tiles, out, wv, evs2, dc, t):
        return Pp.vop(lambda e, o=out, x=tiles[0], y=tiles[1]:
                      e.tensor_tensor(o, x, y, ALU.mult), waits=wv, inc=evs2)
    emit_ew(P, "l3h1", [a1, b1], h1, mbody, 4096)
    emit_proj(P, "l3h1p", P.dram("L3_ffn_h1_w", [4096, 4096], F32R, kind="ExternalInput"),
              h1, a2, 4096, 4096,
              epi={"func": AF.Gelu, "scale_cols": P.dram("L3_ffn_h1_s", [4096], F32R,
                                                         kind="ExternalInput")})
    emit_proj(P, "l3h2p", P.dram("L3_ffn_h2_w", [4096, 4096], F32R, kind="ExternalInput"),
              a2, a3, 4096, 4096,
              epi={"func": AF.Gelu, "scale_cols": P.dram("L3_ffn_h2_s", [4096], F32R,
                                                         kind="ExternalInput")})
    emit_proj(P, "l3w3", P.dram("L3_ffn_w3_w", [4096, DM], F32R, kind="ExternalInput"),
              a3, g3, 4096, DM,
              epi={"func": AF.Silu, "scale_cols": P.dram("L3_ffn_w3_s", [DM], F32R,
                                                         kind="ExternalInput")})
    emit_xupdate(P, "l3xu", x_in, s3, g3, x_out)


def _prep_inputs(inputs):
    ins = {}
    f = np.ascontiguousarray
    ins["L0_attn_Wq"] = f(inputs["L0_attn_Wq"].T)
    ins["L0_attn_Wk"] = f(inputs["L0_attn_Wk"].T)
    ins["L0_attn_Wv"] = f(inputs["L0_attn_Wv"].T)
    ins["L0_attn_Wg"] = f(inputs["L0_attn_Wg"].T)
    ins["L0_attn_Wo"] = f(inputs["L0_attn_Wo"].T)
    ins["L0_attn_g_norm_w"] = f(inputs["L0_attn_g_norm_w"])
    ins["L0_ffn_w_gate"] = f(inputs["L0_ffn_w_gate"].T)
    ins["L0_ffn_w_up"] = f(inputs["L0_ffn_w_up"].T)
    ins["L0_ffn_w_down"] = f(inputs["L0_ffn_w_down"].T)
    ins["L1_attn_context"] = f(inputs["L1_attn_context"].reshape(T, 768).T)
    for k in ("L1_attn_Wq", "L1_attn_Wk", "L1_attn_Wv", "L1_attn_Wout", "L1_attn_bout",
              "L1_ffn_w_gate", "L1_ffn_w_up", "L1_ffn_w_down",
              "L2_attn_Wq", "L2_attn_Wk", "L2_attn_Wv", "L2_attn_Wo",
              "L2_ffn_b1", "L2_ffn_b2", "L2_ffn_bs1", "L2_ffn_bs2"):
        ins[k] = f(inputs[k])
    ins["L2_ffn_w_router"] = f(inputs["L2_ffn_w_router"].T)
    ins["L2_ffn_W1"] = f(inputs["L2_ffn_W1"].transpose(0, 2, 1).reshape(8 * 1024, 4096))
    ins["L2_ffn_W2"] = f(inputs["L2_ffn_W2"].transpose(0, 2, 1).reshape(8 * 4096, 1024))
    ins["L2_ffn_Ws1"] = f(inputs["L2_ffn_Ws1"].T)
    ins["L2_ffn_Ws2"] = f(inputs["L2_ffn_Ws2"].T)
    for k in ("L3_attn_in_proj_w", "L3_attn_conv_w", "L3_attn_conv_b",
              "L3_attn_x_proj_w", "L3_attn_dt_proj_w", "L3_attn_dt_proj_b",
              "L3_attn_A_log", "L3_attn_D", "L3_attn_out_proj_w",
              "L3_attn_ln_g", "L3_attn_ln_b",
              "L3_ffn_w1_s", "L3_ffn_w2_s", "L3_ffn_h1_s", "L3_ffn_h2_s", "L3_ffn_w3_s"):
        ins[k] = f(inputs[k])
    for k in ("L3_ffn_w1_w", "L3_ffn_w2_w", "L3_ffn_h1_w", "L3_ffn_h2_w", "L3_ffn_w3_w"):
        ins[k] = f(inputs[k].T)
    return ins


LAYER_EMITTERS = {0: emit_layer0, 1: emit_layer1, 2: emit_layer2, 3: emit_layer3}


def build_layers(layers=(0,), debug=False):
    nc = bass.Bass("TRN2", target_bir_lowering=False)
    P = Prog(nc, debug=debug)
    x_in = P.dram("x_in", [DM, T], F32R, kind="ExternalInput")
    cur = x_in
    for i, l in enumerate(layers):
        if i == len(layers) - 1:
            out = P.dram("x_out", [DM, T], F32R, kind="ExternalOutput")
        else:
            out = P.dram(f"x_mid{i}", [DM, T], F32R)
        LAYER_EMITTERS[l](P, cur, out)
        cur = out
    P.finish()
    return nc, P


def build_l0(debug=False):
    return build_layers((0,), debug)


def kernel(**inputs):
    x = inputs["L0_attn_x"]
    shared = _prep_inputs(inputs)
    xs = [np.ascontiguousarray(x[c].T) for c in range(NCORES)]
    for layer in (0, 1, 2, 3):
        nc, P = build_layers((layer,))
        in_maps = []
        for c in range(NCORES):
            m = dict(shared)
            m["x_in"] = xs[c]
            in_maps.append(m)
        res = run_bass_kernel_spmd(nc, in_maps, core_ids=list(range(NCORES)))
        xs = [r["x_out"] for r in res.results]
    return np.stack([np.ascontiguousarray(xc.T) for xc in xs])
